# version 08
# baseline (speedup 1.0000x reference)
import numpy as np
import ml_dtypes
import concourse.bass as bass
import concourse.mybir as mybir
from concourse.bass_utils import run_bass_kernel_spmd

F32 = mybir.dt.float32
BF16 = mybir.dt.bfloat16
U8 = mybir.dt.uint8
AF = mybir.ActivationFunctionType
ALU = mybir.AluOpType
AX = mybir.AxisListType
BF = ml_dtypes.bfloat16

B, T, D = 2, 16384, 2048
HEADS, HIDDEN = 16, 8192
SEG = 4096
NCH = 32
NGRP = 8
SLOPES = 2.0 ** (-(np.arange(16, dtype=np.float64) + 1) / 2)

MAGIC = 12582912.0
C1 = 6.28125
C2 = float(np.float64(2 * np.pi) - 6.28125)
INV2PI = float(1.0 / (2 * np.pi))
TB = 2048

ARENA_BYTES = 212800

DEBUG_OUTS = set()
STOP_PHASE = None

ENGS = ("tensor", "vector", "scalar", "gpsimd", "sync")


class Ctr:
    __slots__ = ("h", "n")

    def __init__(self, h):
        self.h = h
        self.n = 0


class KB:
    def __init__(self):
        self.ops = {e: [] for e in ENGS}

    def op(self, eng, fn, waits=(), inc=None, n=1):
        tgt = None
        if inc is not None:
            inc.n += n
            tgt = inc.n
        ws = tuple((c.h, v) for (c, v) in waits if c is not None and v)

        def run(e, fn=fn, ws=ws, inc=inc, n=n):
            for (h, v) in ws:
                e.wait_ge(h, v)
            ins = fn(e)
            if inc is not None:
                if ins is None:
                    e.sem_inc(inc.h, n)
                else:
                    ins.then_inc(inc.h, n)
        self.ops[eng].append(run)
        return tgt

    def barrier(self, ctrs):
        snap = tuple((c.h, c.n) for c in ctrs if c.n > 0)
        for e in ENGS:
            def run(eng, snap=snap):
                for (h, v) in snap:
                    eng.wait_ge(h, v)
            self.ops[e].append(run)


def make_emit_ln(VE, CE, PE):
    def emit_ln(kb, r_sb, ln, csq_ce, out_sb, apply_waits):
        ce_val, sq_ap = csq_ce
        vsum = kb.op("vector", lambda v: v.reduce_sum(
            ln[:, 0:1], r_sb, axis=AX.X), inc=VE)
        vssq = kb.op("vector", lambda v: v.reduce_sum(
            ln[:, 1:2], sq_ap, axis=AX.X), waits=[(CE, ce_val)], inc=VE)

        def pool(fn, waits=(), gap=1024):
            def f(g):
                g.nop(cycle_cnt=gap)
                return fn(g)
            return kb.op("gpsimd", f, waits=waits, inc=PE)
        pool(lambda g: g.tensor_scalar(
            ln[:, 2:3], ln[:, 0:1], 1.0 / 2048, None, ALU.mult),
            waits=[(VE, vssq)])
        pool(lambda g: g.tensor_scalar(
            ln[:, 3:4], ln[:, 1:2], 1.0 / 2048, None, ALU.mult))
        pool(lambda g: g.tensor_tensor(
            ln[:, 4:5], ln[:, 2:3], ln[:, 2:3], ALU.mult))
        p4 = pool(lambda g: g.tensor_tensor(
            ln[:, 5:6], ln[:, 3:4], ln[:, 4:5], ALU.subtract))

        def c2f(sc):
            sc.nop(cycle_cnt=1024)
            return sc.activation(ln[:, 6:7], ln[:, 5:6], AF.Sqrt, bias=1e-5)
        c2 = kb.op("scalar", c2f, waits=[(PE, p4)], inc=CE)
        kb.op("vector", lambda v: v.tensor_tensor(
            out_sb, r_sb, ln[:, 2:3].broadcast_to([128, 2048]),
            ALU.subtract), waits=list(apply_waits) + [(PE, p4)], inc=VE)

        def v4f(v):
            v.nop(cycle_cnt=512)
            return v.reciprocal(ln[:, 7:8], ln[:, 6:7])
        kb.op("vector", v4f, waits=[(CE, c2)], inc=VE)

        def appf(v):
            v.nop(cycle_cnt=512)
            return v.tensor_tensor(
                out_sb, out_sb, ln[:, 7:8].broadcast_to([128, 2048]), ALU.mult)
        return kb.op("vector", appf, inc=VE)
    return emit_ln


def build_nc():
    nc = bass.Bass("TRN2", target_bir_lowering=False)

    def dram(name, shape, dtype):
        kind = "ExternalOutput" if name in DEBUG_OUTS else "Internal"
        return nc.dram_tensor(name, shape, dtype, kind=kind)

    x_tm = nc.dram_tensor("x_tm", [SEG, D], F32, kind="ExternalInput").ap()
    x_fmT = nc.dram_tensor("x_fmT", [D, SEG], F32, kind="ExternalInput").ap()
    wqkvT = nc.dram_tensor("wqkvT", [D, 3 * D], F32, kind="ExternalInput").ap()
    wprojT = nc.dram_tensor("wprojT", [D, D], F32, kind="ExternalInput").ap()
    wgT = nc.dram_tensor("wgT", [D, HIDDEN], F32, kind="ExternalInput").ap()
    wuT = nc.dram_tensor("wuT", [D, HIDDEN], F32, kind="ExternalInput").ap()
    wdT = nc.dram_tensor("wdT", [HIDDEN, D], F32, kind="ExternalInput").ap()
    thetaT = nc.dram_tensor("thetaT", [128, 16], F32, kind="ExternalInput").ap()
    gpos_in = nc.dram_tensor("gpos", [128, SEG], F32, kind="ExternalInput").ap()
    qd_in = nc.dram_tensor("qd", [128, 4096], BF16, kind="ExternalInput").ap()
    kd_in = nc.dram_tensor("kd", [16, 128, 2048], BF16, kind="ExternalInput").ap()
    dd_in = nc.dram_tensor("dd", [16, 128, 128], F32, kind="ExternalInput").ap()
    wglob_in = nc.dram_tensor("wglob", [16, NGRP, 128, 512], BF16,
                              kind="ExternalInput").ap()
    crossw_in = nc.dram_tensor("crossw", [4, 128, 4096], BF16,
                               kind="ExternalInput").ap()
    consts_in = nc.dram_tensor("consts", [128, 4], F32, kind="ExternalInput").ap()
    out = nc.dram_tensor("out", [SEG, D], F32, kind="ExternalOutput").ap()

    q_scr = dram("q_scr", [NGRP, 4096, 512], BF16).ap()
    k_scr = dram("k_scr", [NGRP, 4096, 512], BF16).ap()
    kh_scr = dram("kh_scr", [NGRP, 4096, 512], BF16).ap()
    v_scr = dram("v_scr", [NCH, 128, 2048], BF16).ap()
    o_scr = dram("o_scr", [NGRP, 2048, 512], BF16).ap()
    x1f = dram("x1f", [SEG, D], F32).ap()
    x1b = dram("x1b", [SEG, D], BF16).ap()
    h_scr = dram("h_scr", [NCH, 128, 64, 128], BF16).ap()
    part_scr = dram("part_scr", [NCH, 128, D], F32).ap()
    cc_in = dram("cc_in", [128, 4096], BF16)
    cc_gath = dram("cc_gath", [512, 4096], BF16)

    from contextlib import ExitStack
    ctx = ExitStack()
    arena_h = ctx.enter_context(nc.sbuf_tensor("arena", [128, ARENA_BYTES], U8))
    arena = arena_h[:, :]
    pbA = ctx.enter_context(nc.psum_tensor("pbA", [128, 2048], F32))
    pbB = ctx.enter_context(nc.psum_tensor("pbB", [128, 2048], F32))
    pb = [pbA[:, i * 512:(i + 1) * 512] for i in range(4)] + \
         [pbB[:, i * 512:(i + 1) * 512] for i in range(4)]
    ALLC = []

    def newsem(name):
        c = Ctr(ctx.enter_context(nc.semaphore(name)))
        ALLC.append(c)
        return c

    TE = newsem("te")
    VE = newsem("ve")
    CE = newsem("ce")
    PE = newsem("pe")
    CC = newsem("cc")
    OQ = newsem("oq")
    sPRO = newsem("spro")
    sWQ = [newsem("swq0"), newsem("swq1")]
    sKD = [newsem("skd0"), newsem("skd1")]
    sOT = [newsem(f"sot{i}") for i in range(4)]
    sWV = [newsem("swv0"), newsem("swv1")]
    sVO = [newsem("svo0"), newsem("svo1")]
    sMISC = newsem("smisc")
    sQK = [newsem("sqk0"), newsem("sqk1")]
    sVT = [newsem("svt0"), newsem("svt1")]
    sKT = [newsem("skt0"), newsem("skt1")]
    sOG = [newsem("sog0"), newsem("sog1")]
    sCC1 = newsem("scc1")
    sGA = newsem("sga")
    sWP = newsem("swp")
    sO3 = newsem("so3")
    sQ3 = newsem("sq3")
    sWG3 = newsem("swg3")
    sX3 = [newsem("sx30"), newsem("sx31")]
    sX1 = newsem("sx1")
    sXC = newsem("sxc")
    sX1B = newsem("sx1b")
    sXF = newsem("sxf")
    sFW = [newsem("sfw0"), newsem("sfw1")]
    sHW = [newsem("shw0"), newsem("shw1")]
    sWD = newsem("swd")
    sH2 = [newsem("sh20"), newsem("sh21")]
    sLX = [newsem("slx0"), newsem("slx1")]
    sPT = [newsem("spt0"), newsem("spt1")]

    kb = KB()
    emit_ln = make_emit_ln(VE, CE, PE)

    class Arena:
        def __init__(self, base, limit=ARENA_BYTES):
            self.off = base
            self.limit = limit

        def take(self, nbytes):
            o = (self.off + 63) & ~63
            self.off = o + nbytes
            assert self.off <= self.limit, f"arena overflow {self.off}"
            return o

        def f32(self, cols):
            o = self.take(cols * 4)
            return arena[:, o:o + cols * 4].bitcast(F32)

        def f32_3(self, a, b):
            o = self.take(a * b * 4)
            return arena[:, o:o + a * b * 4].bitcast(F32).rearrange(
                "p (a b) -> p a b", a=a)

        def bf(self, cols):
            o = self.take(cols * 2)
            return arena[:, o:o + cols * 2].bitcast(BF16)

        def bf_3(self, a, b):
            o = self.take(a * b * 2)
            return arena[:, o:o + a * b * 2].bitcast(BF16).rearrange(
                "p (a b) -> p a b", a=a)

    P = Arena(0)
    consts_sb = P.f32(4)
    nc.const_aps.register(F32, 0.0, consts_sb[:, 0:1])
    nc.const_aps.register(F32, 1.0, consts_sb[:, 1:2])
    nc.const_aps.register(F32, 1e-5, consts_sb[:, 2:3])
    nc.const_aps.register(F32, 0.5, consts_sb[:, 3:4])
    theta_sb = P.f32(16)
    PERSIST = P.off
    STATE_OFF = (ARENA_BYTES - (4096 * 4 + 4096 * 2)) & ~63
    Ptop = Arena(STATE_OFF)
    state = Ptop.f32(4096)
    state_bf = Ptop.bf(4096)

    A1a = Arena(PERSIST)
    xfm = A1a.bf_3(16, SEG)
    A1B_BASE = A1a.off

    A1b = Arena(A1B_BASE)
    stripQ = A1b.bf(SEG)
    stripK = A1b.bf(TB)
    ang = A1b.f32(TB)
    t1 = A1b.f32(TB)
    t2 = A1b.f32(TB)
    sin_bf = A1b.bf(TB)
    cos_bf = A1b.bf(TB)
    wq_sb = [A1b.bf_3(16, 128), A1b.bf_3(16, 128)]
    kd_sb = A1b.bf(TB)
    gpos_sb = A1b.f32(TB)
    otile = [A1b.bf(TB) for _ in range(3)]

    Av = Arena(A1B_BASE)
    wv_sb = [Av.bf_3(16, 512), Av.bf_3(16, 512)]
    vout = [Av.bf(512), Av.bf(512)]

    kb.op("gpsimd", lambda g: g.dma_start(out=consts_sb, in_=consts_in),
          inc=sPRO, n=16)
    kb.op("gpsimd", lambda g: g.dma_start(out=theta_sb, in_=thetaT),
          inc=sPRO, n=16)
    for i in range(4):
        kb.op("gpsimd", lambda g, i=i: g.dma_start(
            out=xfm[:, i * 4:(i + 1) * 4, :],
            in_=x_fmT[i * 512:(i + 1) * 512, :]
                .rearrange("(a p) t -> p a t", p=128)), inc=sPRO, n=16)
    kb.barrier(ALLC)

    bank_ve = {}
    wq_te = {}
    pool_q_pe = [0, 0]
    pool_k_pe = [0]
    otile_dma = {}
    tile_rot = [0]
    ang_vr = [0]
    sc_pe = [0, 0]
    sq_ce = [0]
    gp_ce = [0]
    kd_pe = [0]
    sGP = newsem("sgp")

    def half_mms(wbi, pset, bank, half, ld, evac_target, pool_guard):
        w = []
        if ld is not None:
            w.append(ld)
        if pool_guard:
            w.append((PE, pool_guard))
        if bank_ve.get(bank):
            w.append((VE, bank_ve[bank]))
        te_last = None
        for ck in range(16):
            for q in range(4):
                def mm(t, ck=ck, q=q, half=half, wbi=wbi, pset=pset):
                    nt = half * 4 + q
                    return t.matmul(pset[:, q * 512:(q + 1) * 512],
                                    wq_sb[wbi][:, ck, :],
                                    xfm[:, ck, nt * 512:(nt + 1) * 512],
                                    start=(ck == 0), stop=(ck == 15))
                te_last = kb.op("tensor", mm,
                                waits=w if (ck == 0 and q == 0) else (),
                                inc=TE if (ck == 15 and q == 3) else None)
        wq_te[wbi] = te_last
        bank_ve[bank] = kb.op("vector", lambda v, evac_target=evac_target,
                              pset=pset: v.tensor_copy(evac_target,
                                                       pset[:, :]),
                              waits=[(TE, te_last)], inc=VE)
        return bank_ve[bank]

    def load_wq(col0, wbi):
        v = None
        for half in range(2):
            w = []
            if half == 0 and wq_te.get(wbi):
                w.append((TE, wq_te[wbi]))
            v = kb.op("gpsimd", lambda g, col0=col0, half=half, wbi=wbi:
                      g.dma_start(
                          out=wq_sb[wbi][:, half * 8:(half + 1) * 8, :],
                          in_=wqkvT[half * 1024:(half + 1) * 1024,
                                    col0:col0 + 128]
                              .rearrange("(a p) o -> p a o", p=128)),
                      waits=w, inc=sWQ[wbi], n=16)
        return (sWQ[wbi], v)

    def strip_write(scr, h, hf, half, buf):
        g0 = half * 4
        row0 = (2 * h + hf) * 128
        dst = scr[g0:g0 + 4, row0:row0 + 128, :].rearrange("g p t -> p g t")
        src = buf.rearrange("p (g t) -> p g t", g=4)
        return lambda e: e.dma_start(out=dst, in_=src)

    def next_tile(extra_waits):
        ti = tile_rot[0] % 3
        tile_rot[0] += 1
        w = list(extra_waits)
        if otile_dma.get(ti):
            w.append(otile_dma[ti])
        return ti, w

    for h in range(HEADS):
        ldq = load_wq(h * 128, 0)
        ldk = load_wq((16 + h) * 128, 1)
        kdl = kb.op("sync", lambda s, h=h: s.dma_start(
            out=kd_sb, in_=kd_in[h, :, :]),
            waits=[(PE, kd_pe[0])] if kd_pe[0] else (), inc=sKD[0], n=16)
        for half in range(2):
            gpl = kb.op("sync", lambda s, half=half: s.dma_start(
                out=gpos_sb, in_=gpos_in[:, half * TB:(half + 1) * TB]),
                waits=[(CE, gp_ce[0])] if gp_ce[0] else (), inc=sGP, n=16)
            c_ang = kb.op("scalar", lambda s, h=h: s.activation(
                ang, gpos_sb, AF.Copy, bias=0.0, scale=theta_sb[:, h:h + 1]),
                waits=[(sGP, gpl), (VE, ang_vr[0])], inc=CE)
            gp_ce[0] = c_ang
            kb.op("vector", lambda v: v.tensor_scalar(
                t1, ang, INV2PI, None, ALU.mult),
                waits=[(CE, c_ang), (CE, sq_ce[0])], inc=VE)
            kb.op("vector", lambda v: v.tensor_scalar(
                t2, t1, MAGIC, -MAGIC, ALU.add, ALU.add), inc=VE)
            kb.op("vector", lambda v: v.scalar_tensor_tensor(
                t1, t2, -C1, ang, ALU.mult, ALU.add), inc=VE)
            v_r = kb.op("vector", lambda v: v.scalar_tensor_tensor(
                ang, t2, -C2, t1, ALU.mult, ALU.add), inc=VE)
            ang_vr[0] = v_r
            c_sin = kb.op("scalar", lambda s: s.activation(
                sin_bf, ang, AF.Sin),
                waits=[(VE, v_r), (PE, sc_pe[0])], inc=CE)
            kb.op("scalar", lambda s: s.activation(
                t1, ang, AF.Sin, scale=0.5), inc=CE)
            c_sq = kb.op("scalar", lambda s: s.activation(
                t2, t1, AF.Square), inc=CE)
            sq_ce[0] = c_sq
            v_cos = kb.op("vector", lambda v: v.tensor_scalar(
                cos_bf, t2, -2.0, 1.0, ALU.mult, ALU.add),
                waits=[(CE, c_sq), (PE, sc_pe[1])], inc=VE)
            q_ev = half_mms(0, pbA, "qa", half, ldq if half == 0 else None,
                            stripQ[:, half * TB:(half + 1) * TB],
                            pool_q_pe[half])
            for hf, trig_bf in ((0, cos_bf), (1, sin_bf)):
                ti, w = next_tile([(VE, q_ev)])
                w.append((VE, v_cos) if hf == 0 else (CE, c_sin))
                p = kb.op("gpsimd", lambda g, ti=ti, trig_bf=trig_bf, half=half:
                          g.tensor_tensor(otile[ti],
                                          stripQ[:, half * TB:(half + 1) * TB],
                                          trig_bf, ALU.mult),
                          waits=w, inc=PE)
                otile_dma[ti] = (sOT[ti], kb.op(
                    "scalar", strip_write(q_scr, h, hf, half, otile[ti]),
                    waits=[(PE, p)], inc=sOT[ti], n=16))
            pool_q_pe[half] = PE.n
            k_ev = half_mms(1, pbB, "kb", half, ldk if half == 0 else None,
                            stripK, pool_k_pe[0])
            for hf, trig_bf in ((0, cos_bf), (1, sin_bf)):
                ti, w = next_tile([(VE, k_ev)])
                w.append((VE, v_cos) if hf == 0 else (CE, c_sin))
                p1 = kb.op("gpsimd", lambda g, ti=ti, trig_bf=trig_bf:
                           g.tensor_tensor(otile[ti], stripK, trig_bf,
                                           ALU.mult), waits=w, inc=PE)
                otile_dma[ti] = (sOT[ti], kb.op(
                    "scalar", strip_write(k_scr, h, hf, half, otile[ti]),
                    waits=[(PE, p1)], inc=sOT[ti], n=16))
                ti2, w2 = next_tile([(sKD[0], kdl)])
                p2 = kb.op("gpsimd", lambda g, ti=ti, ti2=ti2:
                           g.tensor_tensor(otile[ti2], otile[ti], kd_sb,
                                           ALU.mult),
                           waits=w2 + [(PE, p1)], inc=PE)
                otile_dma[ti2] = (sOT[ti2], kb.op(
                    "scalar", strip_write(kh_scr, h, hf, half, otile[ti2]),
                    waits=[(PE, p2)], inc=sOT[ti2], n=16))
            pool_k_pe[0] = PE.n
            sc_pe[0] = PE.n
            sc_pe[1] = PE.n
        kd_pe[0] = PE.n

    vstart = [(VE, VE.n), (CE, CE.n), (PE, PE.n)] + \
        [(c, c.n) for c in sOT if c.n]
    vload = {}
    vev = {}
    vdma = {}
    vte = {}
    for vt in range(4):
        w = list(vstart) if vt == 0 else []
        if vt >= 2:
            w.append((TE, vte[vt - 2]))
        vload[vt] = kb.op("gpsimd", lambda g, vt=vt: g.dma_start(
            out=wv_sb[vt % 2],
            in_=wqkvT[:, 4096 + vt * 512:4096 + (vt + 1) * 512]
                .rearrange("(a p) o -> p a o", p=128)),
            waits=w, inc=sWV[vt % 2], n=16)
        for tc in range(NCH):
            w = []
            if tc == 0:
                w.append((sWV[vt % 2], vload[vt]))
            bk = tc % 2
            pv = vev.get((vt, tc - 2)) or (vev.get((vt - 1, NCH - 2 + tc))
                                           if vt > 0 else bank_ve.get(bk))
            if pv:
                w.append((VE, pv))
            te_last = None
            for ck in range(16):
                def mm(t, ck=ck, bk=bk, tc=tc, vt=vt):
                    return t.matmul(pb[bk][:, :],
                                    xfm[:, ck, tc * 128:(tc + 1) * 128],
                                    wv_sb[vt % 2][:, ck, :],
                                    start=(ck == 0), stop=(ck == 15))
                te_last = kb.op("tensor", mm, waits=w if ck == 0 else (),
                                inc=TE if ck == 15 else None)
            vte[vt] = te_last
            w2 = [(TE, te_last)]
            if vdma.get((vt, tc - 2)):
                w2.append(vdma[(vt, tc - 2)])
            vev[(vt, tc)] = kb.op("vector", lambda v, tc=tc: v.tensor_copy(
                vout[tc % 2], pb[tc % 2][:, :]), waits=w2, inc=VE)
            vdma[(vt, tc)] = (sVO[tc % 2], kb.op(
                "scalar", lambda s, vt=vt, tc=tc: s.dma_start(
                    out=v_scr[tc, :, vt * 512:(vt + 1) * 512], in_=vout[tc % 2]),
                waits=[(VE, vev[(vt, tc)])], inc=sVO[tc % 2], n=16))

    kb.barrier(ALLC)
    if STOP_PHASE == "A1":
        return finish(nc, kb, ctx)

    A2 = Arena(PERSIST, STATE_OFF)
    qh_sb = [A2.bf_3(32, 256), A2.bf_3(32, 256)]
    khg_sb = [A2.bf_3(32, 256), A2.bf_3(32, 256)]
    vt_sb = [A2.bf(2048), A2.bf(2048)]
    kt_sb = [A2.bf(4096), A2.bf(4096)]
    qhat_sb = [A2.bf_3(32, 128), A2.bf_3(32, 128)]
    sT_sb = [A2.bf(128), A2.bf(128)]
    ogr_sb = [A2.bf_3(16, 512), A2.bf_3(16, 512)]
    dd_sb = A2.f32_3(16, 128)
    qd_sb = A2.bf_3(32, 128)

    ms1 = kb.op("vector", lambda v: v.memset(state, 0.0), inc=VE)
    ms2 = kb.op("vector", lambda v: v.memset(state_bf, 0.0), inc=VE)
    kb.op("sync", lambda s: s.dma_start(
        out=dd_sb, in_=dd_in.rearrange("h j i -> j h i")), inc=sMISC, n=16)
    consts_sa = (sMISC, kb.op("sync", lambda s: s.dma_start(
        out=qd_sb, in_=qd_in.rearrange("p (a b) -> p a b", a=32)),
        inc=sMISC, n=16))

    ld_sa = {}
    te_chunk = {}
    ve_stbf = {}
    ve_stt = {}
    ve_sT = {}
    ve_qhat = {}
    ce_oev = {}
    te_sc = {}
    te_o = {}
    te_kv = {}
    sb_ogrp = {}

    def prevh(d, c, h, back=2):
        if h - back >= 0:
            return d.get((c, h - back))
        if c > 0:
            return d.get((c - 1, h - back + HEADS))
        return None

    def issue_loads(c):
        g = c // 4
        if c % 2 == 0:
            hg = c // 2
            w = []
            if te_chunk.get(c - 3):
                w.append((TE, te_chunk[c - 3]))
            for scr, tgt in ((q_scr, qh_sb), (k_scr, khg_sb)):
                half = (c % 4) // 2
                for part in range(4):
                    v = kb.op("sync", lambda s, g=g, half=half, part=part,
                              scr=scr, tgt=tgt, hg=hg: s.dma_start(
                                  out=tgt[hg % 2][:, part * 8:(part + 1) * 8, :],
                                  in_=scr[g, part * 1024:(part + 1) * 1024,
                                          half * 256:(half + 1) * 256]
                                      .rearrange("(a p) t -> p a t", p=128)),
                              waits=w if part == 0 else (),
                              inc=sQK[hg % 2], n=16)
                ld_sa[("qk", hg)] = (sQK[hg % 2], v)
        w2 = []
        if te_chunk.get(c - 2):
            w2.append((TE, te_chunk[c - 2]))
        ld_sa[("v", c)] = (sVT[c % 2], kb.op("sync", lambda s, c=c: s.dma_start(
            out=vt_sb[c % 2], in_=v_scr[c, :, :]), waits=w2,
            inc=sVT[c % 2], n=16))
        ld_sa[("kt", c)] = (sKT[c % 2], kb.op("sync", lambda s, c=c: s.dma_start(
            out=kt_sb[c % 2],
            in_=kh_scr[c // 4, :, (c % 4) * 128:(c % 4 + 1) * 128],
            transpose=True), inc=sKT[c % 2], n=16))

    issue_loads(0)
    issue_loads(1)

    for c in range(NCH):
        g, hg, cin = c // 4, c // 2, c % 4
        csl = slice((c % 2) * 128, (c % 2) * 128 + 128)
        hgb = hg % 2
        wq = [ld_sa[("qk", hg)]]
        if c == 0:
            wq.append(consts_sa)
        if te_chunk.get(c - 2):
            wq.append((TE, te_chunk[c - 2]))
        ve_qhat[c] = kb.op("vector", lambda v, c=c, csl=csl, hgb=hgb:
                           v.tensor_tensor(qhat_sb[c % 2],
                                           qh_sb[hgb][:, :, csl],
                                           qd_sb, ALU.mult),
                           waits=wq, inc=VE)
        for h in range(HEADS):
            w = []
            if h == 0:
                w.append(ld_sa[("qk", hg)])
            pv = prevh(ve_sT, c, h)
            if pv:
                w.append((VE, pv))
            for half in range(2):
                def mm(t, h=h, half=half, csl=csl, hgb=hgb):
                    return t.matmul(pb[h % 2][:, 0:128],
                                    khg_sb[hgb][:, 2 * h + half, csl],
                                    qh_sb[hgb][:, 2 * h + half, csl],
                                    start=(half == 0), stop=(half == 1))
                te_sc[(c, h)] = kb.op("tensor", mm, waits=w if half == 0 else (),
                                      inc=TE if half == 1 else None)
            wv = [(TE, te_sc[(c, h)])]
            pv = prevh(te_o, c, h)
            if pv:
                wv.append((TE, pv))
            if c == 0 and h < 2:
                wv.append(consts_sa)
            ve_sT[(c, h)] = kb.op("vector", lambda v, h=h: v.tensor_tensor(
                sT_sb[h % 2], pb[h % 2][:, 0:128], dd_sb[:, h, :], ALU.mult),
                waits=wv, inc=VE)
            w = [(VE, ve_sT[(c, h)])]
            if h == 0:
                w.append(ld_sa[("v", c)])
                w.append((VE, ve_qhat[c]))
            pce = prevh(ce_oev, c, h)
            if pce:
                w.append((CE, pce))
            kb.op("tensor", lambda t, h=h, c=c: t.matmul(
                pb[2 + h % 2][:, 0:128],
                vt_sb[c % 2][:, h * 128:(h + 1) * 128],
                sT_sb[h % 2], start=True, stop=False), waits=w)
            w2 = []
            if c == 0 and h == 0:
                w2.append((VE, ms2))
            if ve_stbf.get((c - 1, h)):
                w2.append((VE, ve_stbf[(c - 1, h)]))
            for half in range(2):
                def mm(t, h=h, c=c, half=half):
                    return t.matmul(
                        pb[2 + h % 2][:, 0:128],
                        state_bf[:, h * 256 + half * 128:
                                 h * 256 + (half + 1) * 128],
                        qhat_sb[c % 2][:, 2 * h + half, :],
                        start=False, stop=(half == 1))
                te_o[(c, h)] = kb.op("tensor", mm, waits=w2 if half == 0 else (),
                                     inc=TE if half == 1 else None)
            woe = [(TE, te_o[(c, h)])]
            if cin == 0 and h == 0 and sb_ogrp.get(g - 2):
                woe.append(sb_ogrp[g - 2])
            ce_oev[(c, h)] = kb.op("scalar", lambda s, h=h, g=g, cin=cin:
                                   s.copy(ogr_sb[g % 2][:, h,
                                                        cin * 128:(cin + 1) * 128],
                                          pb[2 + h % 2][:, 0:128]),
                                   waits=woe, inc=CE)
            w = []
            if h == 0:
                w.append(ld_sa[("kt", c)])
            pv = prevh(ve_stt, c, h)
            if pv:
                w.append((VE, pv))
            for half in range(2):
                def mm(t, h=h, c=c, half=half):
                    return t.matmul(
                        pb[4 + h % 2][:, half * 128:(half + 1) * 128],
                        kt_sb[c % 2][:, (2 * h + half) * 128:
                                     (2 * h + half + 1) * 128],
                        vt_sb[c % 2][:, h * 128:(h + 1) * 128],
                        start=True, stop=True)
                te_kv[(c, h)] = kb.op("tensor", mm, waits=w if half == 0 else (),
                                      inc=TE if half == 1 else None)
            bd = float(np.exp(-SLOPES[h] * 128.0))
            wst = [(TE, te_kv[(c, h)])]
            if c == 0 and h == 0:
                wst.append((VE, ms1))
            ve_stt[(c, h)] = kb.op("vector", lambda v, h=h, bd=bd:
                                   v.scalar_tensor_tensor(
                                       state[:, h * 256:(h + 1) * 256],
                                       state[:, h * 256:(h + 1) * 256], bd,
                                       pb[4 + h % 2][:, 0:256],
                                       ALU.mult, ALU.add),
                                   waits=wst, inc=VE)
            ve_stbf[(c, h)] = kb.op("vector", lambda v, h=h: v.tensor_copy(
                state_bf[:, h * 256:(h + 1) * 256],
                state[:, h * 256:(h + 1) * 256]),
                waits=[(TE, te_o[(c, h)])], inc=VE)
        te_chunk[c] = TE.n
        if cin == 3:
            wog = [(CE, ce_oev[(c, 15)])]
            v = None
            for part in range(2):
                v = kb.op("scalar", lambda s, g=g, part=part: s.dma_start(
                    out=o_scr[g, part * 1024:(part + 1) * 1024, :]
                        .rearrange("(a p) t -> p a t", p=128),
                    in_=ogr_sb[g % 2][:, part * 8:(part + 1) * 8, :]),
                    waits=wog if part == 0 else (), inc=sOG[g % 2], n=16)
            sb_ogrp[g] = (sOG[g % 2], v)
        if c + 2 < NCH:
            issue_loads(c + 2)

    kb.barrier(ALLC)
    if STOP_PHASE == "A2":
        return finish(nc, kb, ctx)

    kb.op("sync", lambda s: s.dma_start(out=cc_in.ap(), in_=state_bf),
          inc=sCC1, n=16)
    if "dbg_state" in DEBUG_OUTS:
        dbg_state = nc.dram_tensor("dbg_state", [128, 4096], F32,
                                   kind="ExternalOutput").ap()
        kb.op("sync", lambda s: s.dma_start(out=dbg_state, in_=state),
              inc=sCC1, n=16)
    kb.op("gpsimd", lambda g: g.collective_compute(
        "AllGather", ALU.bypass,
        replica_groups=[[0, 1, 2, 3], [4, 5, 6, 7]],
        ins=[cc_in.ap().opt()], outs=[cc_gath.ap().opt()]),
        waits=[(sCC1, sCC1.n)], inc=CC)

    CA = Arena(PERSIST, STATE_OFF)
    gath_sb = [CA.bf(4096) for _ in range(4)]
    cw_sb = [CA.bf(4096) for _ in range(4)]
    tmp_cc = CA.f32(4096)

    for r in range(4):
        kb.op("sync", lambda s, r=r: s.dma_start(
            out=gath_sb[r], in_=cc_gath.ap()[r * 128:(r + 1) * 128, :]),
            waits=[(CC, 1)] if r == 0 else (), inc=sGA, n=16)
        kb.op("sync", lambda s, r=r: s.dma_start(
            out=cw_sb[r], in_=crossw_in[r, :, :]), inc=sGA, n=16)
    gl = sGA.n
    pv = kb.op("vector", lambda v: v.tensor_tensor(
        state, gath_sb[0], cw_sb[0], ALU.mult), waits=[(sGA, gl)], inc=VE)
    for r in range(1, 4):
        pv = kb.op("vector", lambda v, r=r: v.tensor_tensor(
            tmp_cc, gath_sb[r], cw_sb[r], ALU.mult), waits=[(VE, pv)], inc=VE)
        pv = kb.op("vector", lambda v: v.tensor_tensor(
            state, state, tmp_cc, ALU.add), waits=[(VE, pv)], inc=VE)
    kb.op("vector", lambda v: v.tensor_copy(state_bf, state),
          waits=[(VE, pv)], inc=VE)
    kb.barrier(ALLC)
    if STOP_PHASE == "CC":
        return finish(nc, kb, ctx)

    A3 = Arena(PERSIST, STATE_OFF)
    wpT_sb = A3.bf_3(16, 2048)
    o3_sb = A3.bf_3(16, 512)
    q3_sb = A3.bf_3(32, 512)
    wg3_sb = A3.bf_3(16, 512)
    oc_bf = [A3.bf(512), A3.bf(512)]
    x3_sb = [A3.f32(2048), A3.f32(2048)]
    r3_sb = A3.f32(2048)
    sq3_sb = A3.f32(2048)
    x13_sb = A3.f32(2048)
    x13b_sb = [A3.bf(2048), A3.bf(2048)]
    ln1 = A3.f32(12)

    for part in range(4):
        kb.op("gpsimd", lambda g, part=part: g.dma_start(
            out=wpT_sb[:, part * 4:(part + 1) * 4, :],
            in_=wprojT[part * 512:(part + 1) * 512, :]
                .rearrange("(a p) o -> p a o", p=128)), inc=sWP, n=16)
    wp_sg = sWP.n

    ocbf_pe = {}
    vr_tc = {}
    x1dma = {}
    x1bdma = [0, 0]
    ca_prev = [0]
    dbg_o3 = dbg_r = None
    if "dbg_o3" in DEBUG_OUTS:
        dbg_o3 = nc.dram_tensor("dbg_o3", [NGRP, 16, 128, 512], BF16,
                                kind="ExternalOutput").ap()
    if "dbg_r" in DEBUG_OUTS:
        dbg_r = nc.dram_tensor("dbg_r", [SEG, D], F32,
                               kind="ExternalOutput").ap()
    dbg_ln = None
    dbg_sa = [0]
    if "dbg_ln" in DEBUG_OUTS:
        dbg_ln = nc.dram_tensor("dbg_ln", [NCH, 128, 12], F32,
                                kind="ExternalOutput").ap()
    for g in range(NGRP):
        wld = []
        if g > 0:
            wld.append((TE, TE.n))
            wld.append((PE, PE.n))
        lo = kb.op("sync", lambda s, g=g: s.dma_start(
            out=o3_sb, in_=o_scr[g, :, :].rearrange("(a p) t -> p a t", p=128)),
            waits=wld, inc=sO3, n=16)
        lq = kb.op("sync", lambda s, g=g: s.dma_start(
            out=q3_sb, in_=q_scr[g, :, :].rearrange("(a p) t -> p a t", p=128)),
            inc=sQ3, n=16)
        lw = kb.op("sync", lambda s, g=g: s.dma_start(
            out=wg3_sb, in_=wglob_in[:, g, :, :].rearrange("h p j -> p h j")),
            inc=sWG3, n=16)
        pq = None
        for fc in range(32):
            pq = kb.op("gpsimd", lambda g2, fc=fc: g2.tensor_tensor(
                q3_sb[:, fc, :], q3_sb[:, fc, :], wg3_sb[:, fc // 2, :],
                ALU.mult),
                waits=[(sWG3, lw), (sQ3, lq)] if fc == 0 else (), inc=PE)
        pe_add = None
        for h in range(HEADS):
            w = [(PE, pq)] if h == 0 else []
            if ocbf_pe.get(h % 2):
                w.append((PE, ocbf_pe[h % 2]))
            tlast = None
            for half in range(2):
                def mm(t, h=h, half=half):
                    return t.matmul(
                        pb[h % 2][:, :],
                        state_bf[:, h * 256 + half * 128:
                                 h * 256 + (half + 1) * 128],
                        q3_sb[:, 2 * h + half, :],
                        start=(half == 0), stop=(half == 1))
                tlast = kb.op("tensor", mm, waits=w if half == 0 else (),
                              inc=TE if half == 1 else None)
            cev = kb.op("scalar", lambda s, h=h: s.copy(
                oc_bf[h % 2], pb[h % 2][:, :]),
                waits=[(TE, tlast), (sO3, lo)], inc=CE)
            pe_add = kb.op("gpsimd", lambda g2, h=h: g2.tensor_tensor(
                o3_sb[:, h, :], o3_sb[:, h, :], oc_bf[h % 2], ALU.add),
                waits=[(CE, cev)], inc=PE)
            ocbf_pe[h % 2] = pe_add
        if dbg_o3 is not None:
            kb.op("sync", lambda s, g=g: s.dma_start(
                out=dbg_o3[g].rearrange("a p t -> p a t"), in_=o3_sb),
                waits=[(PE, pe_add)], inc=sMISC, n=16)
        for tc in range(4):
            tok0 = g * 512 + tc * 128
            wlx = []
            if vr_tc.get(tc % 2):
                wlx.append((VE, vr_tc[tc % 2]))
            lx = kb.op("sync", lambda s, tok0=tok0: s.dma_start(
                out=x3_sb[(tok0 // 128) % 2], in_=x_tm[tok0:tok0 + 128, :]),
                waits=wlx, inc=sX3[(tok0 // 128) % 2], n=16)
            for og in range(4):
                w = []
                if og == 0 and tc == 0:
                    w += [(PE, pe_add), (sWP, wp_sg)]
                if og == 0 and vr_tc.get("last"):
                    w.append((VE, vr_tc["last"]))
                for ck in range(16):
                    def mm(t, og=og, ck=ck, tc=tc):
                        return t.matmul(pb[4 + og][:, :],
                                        o3_sb[:, ck, tc * 128:(tc + 1) * 128],
                                        wpT_sb[:, ck, og * 512:(og + 1) * 512],
                                        start=(ck == 0), stop=(ck == 15))
                    kb.op("tensor", mm, waits=w if ck == 0 else (),
                          inc=TE if ck == 15 else None)
            projte = TE.n
            vr = None
            for og in range(4):
                w = []
                if og == 0:
                    w += [(TE, projte), (sX3[(tok0 // 128) % 2], lx)]
                    if ca_prev[0]:
                        w.append((CE, ca_prev[0]))
                    if dbg_sa[0]:
                        w.append(dbg_sa[0])
                vr = kb.op("vector", lambda v, og=og, tc=tc, tok0=tok0:
                           v.tensor_tensor(
                               r3_sb[:, og * 512:(og + 1) * 512],
                               pb[4 + og][:, :],
                               x3_sb[(tok0 // 128) % 2][:, og * 512:(og + 1) * 512],
                               ALU.add), waits=w, inc=VE)
            vr_tc[tc % 2] = vr
            vr_tc["last"] = vr
            if dbg_r is not None:
                kb.op("sync", lambda s, tok0=tok0: s.dma_start(
                    out=dbg_r[tok0:tok0 + 128, :], in_=r3_sb),
                    waits=[(VE, vr)], inc=sMISC, n=16)
            def csqf(sc):
                sc.nop(cycle_cnt=1024)
                return sc.activation(sq3_sb, r3_sb, AF.Square)
            csq = kb.op("scalar", csqf, waits=[(VE, vr)], inc=CE)
            ca_prev[0] = csq
            wca = []
            if x1dma.get(0):
                wca.append((sX1, x1dma[0]))
            ca = emit_ln(kb, r3_sb, ln1, (csq, sq3_sb), x13_sb, wca)
            if dbg_ln is not None:
                dbg_sa[0] = (sMISC, kb.op("sync", lambda s, g=g, tc=tc:
                    s.dma_start(out=dbg_ln[g * 4 + tc], in_=ln1),
                    waits=[(VE, ca)], inc=sMISC, n=16))
            x1dma[0] = kb.op("scalar", lambda s, tok0=tok0: s.dma_start(
                out=x1f[tok0:tok0 + 128, :], in_=x13_sb),
                waits=[(VE, ca)], inc=sX1, n=16)
            par = (tok0 // 128) % 2
            cab = kb.op("vector", lambda v, par=par: v.tensor_copy(
                x13b_sb[par], x13_sb),
                waits=[x1bdma[par]] if x1bdma[par] else (), inc=VE)
            x1bdma[par] = (sXC if par == 0 else sX1B,
                           kb.op("scalar", lambda s, tok0=tok0, par=par:
                                 s.dma_start(out=x1b[tok0:tok0 + 128, :],
                                             in_=x13b_sb[par]),
                                 waits=[(VE, cab)],
                                 inc=sXC if par == 0 else sX1B, n=16))

    kb.barrier(ALLC)
    if STOP_PHASE == "A34":
        return finish(nc, kb, ctx)

    kb.barrier(ALLC)

    F1 = Arena(PERSIST)
    x1fm = F1.bf_3(16, SEG)
    wg_sb = [F1.bf_3(16, 128) for _ in range(2)]
    wu_sb = [F1.bf_3(16, 128) for _ in range(2)]
    g_bf = [F1.bf(2048), F1.bf(2048)]
    u_bf = [F1.bf(2048), F1.bf(2048)]
    hstrip = [F1.bf(SEG), F1.bf(SEG)]

    for fc in range(16):
        kb.op("sync", lambda s, fc=fc: s.dma_start(
            out=x1fm[:, fc, :], in_=x1b[:, fc * 128:(fc + 1) * 128],
            transpose=True), inc=sXF, n=16)
    x1fm_sa = sXF.n

    f1_te = {}
    f1_ce = {}
    f1_ve = {}
    f1_pe = {}
    f1_hdma = {}

    def f1_prev(d, hc, half):
        if half - 1 >= 0:
            return d.get((hc, half - 1))
        if hc > 0:
            return d.get((hc - 1, 1))
        return None

    for hc in range(64):
        w = []
        if hc >= 2:
            w.append((TE, f1_te[(hc - 2, 1)]))
        wl = None
        for wsrc, tgt in ((wgT, wg_sb), (wuT, wu_sb)):
            for half in range(2):
                wl = kb.op("gpsimd", lambda g, wsrc=wsrc, tgt=tgt, hc=hc,
                           half=half: g.dma_start(
                               out=tgt[hc % 2][:, half * 8:(half + 1) * 8, :],
                               in_=wsrc[half * 1024:(half + 1) * 1024,
                                        hc * 128:(hc + 1) * 128]
                                   .rearrange("(a p) o -> p a o", p=128)),
                           waits=w if (half == 0 and wsrc is wgT) else (),
                           inc=sFW[hc % 2], n=16)
        for half in range(2):
            w = []
            if half == 0:
                w.append((sFW[hc % 2], wl))
                if hc == 0:
                    w.append((sXF, x1fm_sa))
            pv = f1_prev(f1_ce, hc, half)
            if pv:
                w.append((CE, pv))
            pv = f1_prev(f1_ve, hc, half)
            if pv:
                w.append((VE, pv))
            te_last = None
            for ck in range(16):
                for q in range(4):
                    def mmg(t, hc=hc, half=half, ck=ck, q=q):
                        nt = half * 4 + q
                        return t.matmul(pb[q][:, :], wg_sb[hc % 2][:, ck, :],
                                        x1fm[:, ck, nt * 512:(nt + 1) * 512],
                                        start=(ck == 0), stop=(ck == 15))
                    kb.op("tensor", mmg, waits=w if (ck == 0 and q == 0) else ())
            for ck in range(16):
                for q in range(4):
                    def mmu(t, hc=hc, half=half, ck=ck, q=q):
                        nt = half * 4 + q
                        return t.matmul(pb[4 + q][:, :],
                                        wu_sb[hc % 2][:, ck, :],
                                        x1fm[:, ck, nt * 512:(nt + 1) * 512],
                                        start=(ck == 0), stop=(ck == 15))
                    te_last = kb.op("tensor", mmu,
                                    inc=TE if (ck == 15 and q == 3) else None)
            f1_te[(hc, half)] = te_last
            wsl = [(TE, te_last)]
            ppe = f1_prev(f1_pe, hc, half)
            if ppe:
                wsl.append((PE, ppe))
            f1_ce[(hc, half)] = kb.op("scalar", lambda s, half=half: s.activation(
                g_bf[half], pbA[:, :], AF.Silu), waits=wsl, inc=CE)
            f1_ve[(hc, half)] = kb.op("vector", lambda v, half=half: v.tensor_copy(
                u_bf[half], pbB[:, :]), waits=wsl, inc=VE)
            wmul = [(CE, f1_ce[(hc, half)]), (VE, f1_ve[(hc, half)])]
            if half == 0 and f1_hdma.get(hc - 2):
                wmul.append((sHW[hc % 2], f1_hdma[hc - 2]))
            f1_pe[(hc, half)] = kb.op("gpsimd", lambda g, hc=hc, half=half:
                                      g.tensor_tensor(
                                          hstrip[hc % 2][:, half * 2048:(half + 1) * 2048],
                                          g_bf[half], u_bf[half], ALU.mult),
                                      waits=wmul, inc=PE)
        wh = [(PE, f1_pe[(hc, 1)])]
        v = None
        for part in range(4):
            v = kb.op("scalar", lambda s, hc=hc, part=part: s.dma_start(
                out=h_scr[part * 8:(part + 1) * 8, :, hc, :]
                    .rearrange("c p t -> p c t"),
                in_=hstrip[hc % 2].rearrange("p (c t) -> p c t", c=32)
                    [:, part * 8:(part + 1) * 8, :]),
                waits=wh if part == 0 else (), inc=sHW[hc % 2], n=16)
        f1_hdma[hc] = v

    kb.barrier(ALLC)
    if STOP_PHASE == "F1":
        return finish(nc, kb, ctx)

    F2 = Arena(PERSIST)
    wd_sb = F2.bf_3(32, 2048)
    h2_sb = [F2.bf_3(32, 128), F2.bf_3(32, 128)]
    pt_sb = F2.f32(2048)
    x1t_sb = [F2.f32(2048), F2.f32(2048)]
    r2_sb = F2.f32(2048)
    sq2_sb = F2.f32(2048)
    o2_sb = F2.f32(2048)
    ln2 = F2.f32(12)

    f2_te = {}
    f2_ev = {}
    pt_dma = {}
    out_dma = [0]
    csq2_prev = [0]
    for Hh in range(2):
        w = []
        if Hh == 1:
            w.append((TE, TE.n))
        wd_parts = []
        for part in range(4):
            wd_parts.append(kb.op(
                "gpsimd", lambda g, Hh=Hh, part=part: g.dma_start(
                    out=wd_sb[:, part * 8:(part + 1) * 8, :],
                    in_=wdT[Hh * 4096 + part * 1024:
                            Hh * 4096 + (part + 1) * 1024, :]
                        .rearrange("(a p) o -> p a o", p=128)),
                waits=w if part == 0 else (), inc=sWD, n=16))
        wdl = wd_parts[-1]
        for tc in range(NCH):
            w = []
            hprev = f2_te.get((Hh, tc - 2)) or \
                (f2_te.get((0, NCH - 2 + tc)) if Hh == 1 else None)
            if hprev:
                w.append((TE, hprev))
            lh = kb.op("sync", lambda s, Hh=Hh, tc=tc: s.dma_start(
                out=h2_sb[tc % 2], in_=h_scr[tc, :, Hh * 32:(Hh + 1) * 32, :]),
                waits=w, inc=sH2[tc % 2], n=16)
            wlx = []
            xprev = f2_ev.get((Hh, tc - 2)) or \
                (f2_ev.get((0, NCH - 2 + tc)) if Hh == 1 else None)
            if xprev:
                wlx.append((VE, xprev))
            if Hh == 0:
                lx = kb.op("sync", lambda s, tc=tc: s.dma_start(
                    out=x1t_sb[tc % 2], in_=x1f[tc * 128:(tc + 1) * 128, :]),
                    waits=wlx, inc=sLX[tc % 2], n=16)
            else:
                lx = kb.op("sync", lambda s, tc=tc: s.dma_start(
                    out=x1t_sb[tc % 2], in_=part_scr[tc, :, :]),
                    waits=wlx + [pt_dma[tc]], inc=sLX[tc % 2], n=16)
            w = [(sH2[tc % 2], lh)]
            pvv = f2_ev.get((Hh, tc - 1)) if tc > 0 else \
                (f2_ev.get((0, NCH - 1)) if Hh == 1 else None)
            if pvv:
                w.append((VE, pvv))
            tl = None
            for hc in range(32):
                mmw = ()
                if hc == 0 and tc == 0:
                    mmw = w[:]
                if tc == 0 and hc % 8 == 0:
                    mmw = list(mmw) + [(sWD, wd_parts[hc // 8])]
                elif hc == 0:
                    mmw = w
                for og in range(4):
                    def mm(t, hc=hc, og=og, tc=tc):
                        return t.matmul(pb[4 + og][:, :], h2_sb[tc % 2][:, hc, :],
                                        wd_sb[:, hc, og * 512:(og + 1) * 512],
                                        start=(hc == 0), stop=(hc == 31))
                    tl = kb.op("tensor", mm,
                               waits=mmw if og == 0 else (),
                               inc=TE if (hc == 31 and og == 3) else None)
            f2_te[(Hh, tc)] = tl
            if Hh == 0:
                vv = None
                for og in range(4):
                    w = []
                    if og == 0:
                        w = [(TE, tl), (sLX[tc % 2], lx)]
                        if pt_dma.get(tc - 1):
                            w.append(pt_dma[tc - 1])
                    vv = kb.op("vector", lambda v, og=og, tc=tc: v.tensor_tensor(
                        pt_sb[:, og * 512:(og + 1) * 512], pb[4 + og][:, :],
                        x1t_sb[tc % 2][:, og * 512:(og + 1) * 512], ALU.add),
                        waits=w, inc=VE)
                f2_ev[(Hh, tc)] = vv
                pt_dma[tc] = (sPT[tc % 2], kb.op(
                    "scalar", lambda s, tc=tc: s.dma_start(
                        out=part_scr[tc, :, :], in_=pt_sb), waits=[(VE, vv)],
                    inc=sPT[tc % 2], n=16))
            else:
                vv = None
                for og in range(4):
                    w = []
                    if og == 0:
                        w = [(TE, tl), (sLX[tc % 2], lx)]
                        if csq2_prev[0]:
                            w.append((CE, csq2_prev[0]))
                    vv = kb.op("vector", lambda v, og=og, tc=tc: v.tensor_tensor(
                        r2_sb[:, og * 512:(og + 1) * 512], pb[4 + og][:, :],
                        x1t_sb[tc % 2][:, og * 512:(og + 1) * 512], ALU.add),
                        waits=w, inc=VE)
                f2_ev[(Hh, tc)] = vv
                def csqf2(sc):
                    sc.nop(cycle_cnt=1024)
                    return sc.activation(sq2_sb, r2_sb, AF.Square)
                csq = kb.op("scalar", csqf2, waits=[(VE, vv)], inc=CE)
                csq2_prev[0] = csq
                wca = []
                if out_dma[0]:
                    wca.append((OQ, out_dma[0]))
                ca2 = emit_ln(kb, r2_sb, ln2, (csq, sq2_sb), o2_sb, wca)
                out_dma[0] = kb.op("scalar", lambda s, tc=tc: s.dma_start(
                    out=out[tc * 128:(tc + 1) * 128, :], in_=o2_sb),
                    waits=[(VE, ca2)], inc=OQ, n=16)

    kb.op("sync", lambda s: None, waits=[(OQ, OQ.n)])
    kb.op("scalar", lambda s: None, waits=[(OQ, OQ.n)])
    return finish(nc, kb, ctx)


def finish(nc, kb, ctx):
    with nc.Block() as block:
        @block.tensor
        def _(t):
            for f in kb.ops["tensor"]:
                f(t)

        @block.vector
        def _(v):
            for f in kb.ops["vector"]:
                f(v)

        @block.scalar
        def _(s):
            for f in kb.ops["scalar"]:
                f(s)

        @block.gpsimd
        def _(g):
            for f in kb.ops["gpsimd"]:
                f(g)

        @block.sync
        def _(s):
            for f in kb.ops["sync"]:
                f(s)
    ctx.close()
    return nc


def host_consts(seg_idx):
    s = SLOPES
    jj = np.arange(128, dtype=np.float64)
    qd = np.zeros((32, 128), dtype=np.float64)
    for fc in range(32):
        qd[fc] = np.exp(-s[fc // 2] * (jj + 1.0))
    qd = np.tile(qd.reshape(1, 32 * 128), (128, 1)).astype(BF)

    j2k = np.arange(2048, dtype=np.float64) % 128
    kd = np.zeros((16, 128, 2048), dtype=np.float64)
    for h in range(16):
        kd[h, :, :] = np.exp(-s[h] * (127.0 - j2k))[None, :]
    kd = kd.astype(BF)

    dd = np.zeros((16, 128, 128), dtype=np.float64)
    ii = np.arange(128, dtype=np.float64)
    for h in range(16):
        dif = ii[None, :] - jj[:, None]
        dd[h] = np.where(dif >= 0, np.exp(-s[h] * dif), 0.0)
    dd = dd.astype(np.float32)

    wg = np.zeros((16, NGRP, 128, 512), dtype=np.float64)
    jg = np.arange(512, dtype=np.float64)
    for h in range(16):
        for g in range(NGRP):
            wg[h, g, :, :] = np.exp(-s[h] * (g * 512.0 + jg + 1.0))[None, :]
    wglob = wg.astype(BF)

    cw = np.zeros((4, 128, 4096), dtype=np.float64)
    for sp in range(4):
        if sp < seg_idx:
            for h in range(16):
                cw[sp, :, h * 256:(h + 1) * 256] = np.exp(
                    -4096.0 * s[h] * (seg_idx - sp - 1))
    crossw = cw.astype(BF)

    gp = (seg_idx * 4096.0 + np.arange(4096, dtype=np.float64))
    gpos = np.tile(gp.reshape(1, -1), (128, 1)).astype(np.float32)

    consts = np.zeros((128, 4), dtype=np.float32)
    consts[:, 1] = 1.0
    consts[:, 2] = 1e-5
    consts[:, 3] = 0.5
    return dict(qd=np.ascontiguousarray(qd.reshape(128, 4096)), kd=kd, dd=dd,
                wglob=wglob, crossw=crossw, gpos=gpos, consts=consts)


_NC_CACHE = {}


def make_in_maps(L0_attn_x, L0_attn_w_qkv, L0_attn_w_proj, L0_attn_theta,
                 L0_ffn_w_gate, L0_ffn_w_up, L0_ffn_w_down):
    x = np.asarray(L0_attn_x, dtype=np.float32)
    wq = np.ascontiguousarray(np.asarray(L0_attn_w_qkv, np.float32).T)
    wp = np.ascontiguousarray(np.asarray(L0_attn_w_proj, np.float32).T)
    th = np.ascontiguousarray(
        np.asarray(L0_attn_theta, np.float32).reshape(16, 128).T)
    wgm = np.ascontiguousarray(np.asarray(L0_ffn_w_gate, np.float32).T)
    wum = np.ascontiguousarray(np.asarray(L0_ffn_w_up, np.float32).T)
    wdm = np.ascontiguousarray(np.asarray(L0_ffn_w_down, np.float32).T)
    in_maps = []
    for i in range(8):
        b, sgi = i // 4, i % 4
        xs = np.ascontiguousarray(x[b, sgi * SEG:(sgi + 1) * SEG, :])
        m = dict(x_tm=xs, x_fmT=np.ascontiguousarray(xs.T),
                 wqkvT=wq, wprojT=wp, wgT=wgm, wuT=wum, wdT=wdm, thetaT=th)
        m.update(host_consts(sgi))
        in_maps.append(m)
    return in_maps


def kernel(L0_attn_x, L0_attn_w_qkv, L0_attn_w_proj, L0_attn_theta,
           L0_ffn_w_gate, L0_ffn_w_up, L0_ffn_w_down):
    key = (STOP_PHASE, tuple(sorted(DEBUG_OUTS)))
    if key not in _NC_CACHE:
        _NC_CACHE[key] = build_nc()
    nc = _NC_CACHE[key]
    in_maps = make_in_maps(L0_attn_x, L0_attn_w_qkv, L0_attn_w_proj,
                           L0_attn_theta, L0_ffn_w_gate, L0_ffn_w_up,
                           L0_ffn_w_down)
    res = run_bass_kernel_spmd(nc, in_maps, core_ids=list(range(8)))
    full = np.empty((B, T, D), dtype=np.float32)
    for i in range(8):
        b, sgi = i // 4, i % 4
        full[b, sgi * SEG:(sgi + 1) * SEG, :] = res.results[i]["out"]
    return full


# version 09
# speedup vs baseline: 1.0536x; 1.0536x over previous
import numpy as np
import ml_dtypes
import concourse.bass as bass
import concourse.mybir as mybir
from concourse.bass_utils import run_bass_kernel_spmd

F32 = mybir.dt.float32
BF16 = mybir.dt.bfloat16
U8 = mybir.dt.uint8
AF = mybir.ActivationFunctionType
ALU = mybir.AluOpType
AX = mybir.AxisListType
BF = ml_dtypes.bfloat16

B, T, D = 2, 16384, 2048
HEADS, HIDDEN = 16, 8192
SEG = 4096
NCH = 32
NGRP = 8
SLOPES = 2.0 ** (-(np.arange(16, dtype=np.float64) + 1) / 2)

MAGIC = 12582912.0
C1 = 6.28125
C2 = float(np.float64(2 * np.pi) - 6.28125)
INV2PI = float(1.0 / (2 * np.pi))
TB = 2048

ARENA_BYTES = 212800

DEBUG_OUTS = set()
STOP_PHASE = None

ENGS = ("tensor", "vector", "scalar", "gpsimd", "sync")


class Ctr:
    __slots__ = ("h", "n")

    def __init__(self, h):
        self.h = h
        self.n = 0


class KB:
    def __init__(self):
        self.ops = {e: [] for e in ENGS}

    def op(self, eng, fn, waits=(), inc=None, n=1):
        tgt = None
        if inc is not None:
            inc.n += n
            tgt = inc.n
        ws = tuple((c.h, v) for (c, v) in waits if c is not None and v)

        def run(e, fn=fn, ws=ws, inc=inc, n=n):
            for (h, v) in ws:
                e.wait_ge(h, v)
            ins = fn(e)
            if inc is not None:
                if ins is None:
                    e.sem_inc(inc.h, n)
                else:
                    ins.then_inc(inc.h, n)
        self.ops[eng].append(run)
        return tgt

    def barrier(self, ctrs):
        snap = tuple((c.h, c.n) for c in ctrs if c.n > 0)
        for e in ENGS:
            def run(eng, snap=snap):
                for (h, v) in snap:
                    eng.wait_ge(h, v)
            self.ops[e].append(run)


def make_emit_ln(VE, CE, PE):
    def emit_ln(kb, r_sb, ln, csq_ce, out_sb, apply_waits):
        ce_val, sq_ap = csq_ce
        vsum = kb.op("vector", lambda v: v.reduce_sum(
            ln[:, 0:1], r_sb, axis=AX.X), inc=VE)
        vssq = kb.op("vector", lambda v: v.reduce_sum(
            ln[:, 1:2], sq_ap, axis=AX.X), waits=[(CE, ce_val)], inc=VE)

        def pool(fn, waits=(), gap=1024):
            def f(g):
                g.nop(cycle_cnt=gap)
                return fn(g)
            return kb.op("gpsimd", f, waits=waits, inc=PE)
        pool(lambda g: g.tensor_scalar(
            ln[:, 2:3], ln[:, 0:1], 1.0 / 2048, None, ALU.mult),
            waits=[(VE, vssq)])
        pool(lambda g: g.tensor_scalar(
            ln[:, 3:4], ln[:, 1:2], 1.0 / 2048, None, ALU.mult))
        pool(lambda g: g.tensor_tensor(
            ln[:, 4:5], ln[:, 2:3], ln[:, 2:3], ALU.mult))
        p4 = pool(lambda g: g.tensor_tensor(
            ln[:, 5:6], ln[:, 3:4], ln[:, 4:5], ALU.subtract))

        def c2f(sc):
            sc.nop(cycle_cnt=1024)
            return sc.activation(ln[:, 6:7], ln[:, 5:6], AF.Sqrt, bias=1e-5)
        c2 = kb.op("scalar", c2f, waits=[(PE, p4)], inc=CE)
        kb.op("vector", lambda v: v.tensor_tensor(
            out_sb, r_sb, ln[:, 2:3].broadcast_to([128, 2048]),
            ALU.subtract), waits=list(apply_waits) + [(PE, p4)], inc=VE)

        def v4f(v):
            v.nop(cycle_cnt=512)
            return v.reciprocal(ln[:, 7:8], ln[:, 6:7])
        kb.op("vector", v4f, waits=[(CE, c2)], inc=VE)

        def appf(v):
            v.nop(cycle_cnt=512)
            return v.tensor_tensor(
                out_sb, out_sb, ln[:, 7:8].broadcast_to([128, 2048]), ALU.mult)
        return kb.op("vector", appf, inc=VE)
    return emit_ln


def build_nc():
    nc = bass.Bass("TRN2", target_bir_lowering=False)

    def dram(name, shape, dtype):
        kind = "ExternalOutput" if name in DEBUG_OUTS else "Internal"
        return nc.dram_tensor(name, shape, dtype, kind=kind)

    x_tm = nc.dram_tensor("x_tm", [SEG, D], F32, kind="ExternalInput").ap()
    x_fmT = nc.dram_tensor("x_fmT", [D, SEG], F32, kind="ExternalInput").ap()
    wqkvT = nc.dram_tensor("wqkvT", [D, 3 * D], F32, kind="ExternalInput").ap()
    wprojT = nc.dram_tensor("wprojT", [D, D], F32, kind="ExternalInput").ap()
    wgT = nc.dram_tensor("wgT", [D, HIDDEN], F32, kind="ExternalInput").ap()
    wuT = nc.dram_tensor("wuT", [D, HIDDEN], F32, kind="ExternalInput").ap()
    wdT = nc.dram_tensor("wdT", [HIDDEN, D], F32, kind="ExternalInput").ap()
    thetaT = nc.dram_tensor("thetaT", [128, 16], F32, kind="ExternalInput").ap()
    gpos_in = nc.dram_tensor("gpos", [128, SEG], F32, kind="ExternalInput").ap()
    qd_in = nc.dram_tensor("qd", [128, 4096], BF16, kind="ExternalInput").ap()
    kd_in = nc.dram_tensor("kd", [16, 128, 2048], BF16, kind="ExternalInput").ap()
    dd_in = nc.dram_tensor("dd", [16, 128, 128], F32, kind="ExternalInput").ap()
    wglob_in = nc.dram_tensor("wglob", [16, NGRP, 128, 512], BF16,
                              kind="ExternalInput").ap()
    crossw_in = nc.dram_tensor("crossw", [4, 128, 4096], BF16,
                               kind="ExternalInput").ap()
    consts_in = nc.dram_tensor("consts", [128, 4], F32, kind="ExternalInput").ap()
    out = nc.dram_tensor("out", [SEG, D], F32, kind="ExternalOutput").ap()

    q_scr = dram("q_scr", [NGRP, 4096, 512], BF16).ap()
    k_scr = dram("k_scr", [NGRP, 4096, 512], BF16).ap()
    kh_scr = dram("kh_scr", [NGRP, 4096, 512], BF16).ap()
    v_scr = dram("v_scr", [NCH, 128, 2048], BF16).ap()
    o_scr = dram("o_scr", [NGRP, 2048, 512], BF16).ap()
    x1f = dram("x1f", [SEG, D], F32).ap()
    x1b = dram("x1b", [SEG, D], BF16).ap()
    h_scr = dram("h_scr", [NCH, 128, 64, 128], BF16).ap()
    part_scr = dram("part_scr", [NCH, 128, D], F32).ap()
    cc_in = dram("cc_in", [128, 4096], BF16)
    cc_gath = dram("cc_gath", [512, 4096], BF16)

    from contextlib import ExitStack
    ctx = ExitStack()
    arena_h = ctx.enter_context(nc.sbuf_tensor("arena", [128, ARENA_BYTES], U8))
    arena = arena_h[:, :]
    pbA = ctx.enter_context(nc.psum_tensor("pbA", [128, 2048], F32))
    pbB = ctx.enter_context(nc.psum_tensor("pbB", [128, 2048], F32))
    pb = [pbA[:, i * 512:(i + 1) * 512] for i in range(4)] + \
         [pbB[:, i * 512:(i + 1) * 512] for i in range(4)]
    ALLC = []

    def newsem(name):
        c = Ctr(ctx.enter_context(nc.semaphore(name)))
        ALLC.append(c)
        return c

    TE = newsem("te")
    VE = newsem("ve")
    CE = newsem("ce")
    PE = newsem("pe")
    CC = newsem("cc")
    OQ = newsem("oq")
    sPRO = newsem("spro")
    sWQ = [newsem("swq0"), newsem("swq1")]
    sKD = [newsem("skd0"), newsem("skd1")]
    sOT = [newsem(f"sot{i}") for i in range(4)]
    sWV = [newsem("swv0"), newsem("swv1")]
    sVO = [newsem("svo0"), newsem("svo1")]
    sMISC = newsem("smisc")
    sQK = [newsem("sqk0"), newsem("sqk1")]
    sVT = [newsem("svt0"), newsem("svt1")]
    sKT = [newsem("skt0"), newsem("skt1")]
    sOG = [newsem("sog0"), newsem("sog1")]
    sCC1 = newsem("scc1")
    sGA = newsem("sga")
    sWP = newsem("swp")
    sO3 = newsem("so3")
    sQ3 = newsem("sq3")
    sWG3 = newsem("swg3")
    sX3 = [newsem("sx30"), newsem("sx31")]
    sX1 = newsem("sx1")
    sXC = newsem("sxc")
    sX1B = newsem("sx1b")
    sXF = newsem("sxf")
    sFW = [newsem("sfw0"), newsem("sfw1")]
    sHW = [newsem("shw0"), newsem("shw1")]
    sWD = newsem("swd")
    sH2 = [newsem("sh20"), newsem("sh21")]
    sLX = [newsem("slx0"), newsem("slx1")]
    sPT = [newsem("spt0"), newsem("spt1")]

    kb = KB()
    emit_ln = make_emit_ln(VE, CE, PE)

    class Arena:
        def __init__(self, base, limit=ARENA_BYTES):
            self.off = base
            self.limit = limit

        def take(self, nbytes):
            o = (self.off + 63) & ~63
            self.off = o + nbytes
            assert self.off <= self.limit, f"arena overflow {self.off}"
            return o

        def f32(self, cols):
            o = self.take(cols * 4)
            return arena[:, o:o + cols * 4].bitcast(F32)

        def f32_3(self, a, b):
            o = self.take(a * b * 4)
            return arena[:, o:o + a * b * 4].bitcast(F32).rearrange(
                "p (a b) -> p a b", a=a)

        def bf(self, cols):
            o = self.take(cols * 2)
            return arena[:, o:o + cols * 2].bitcast(BF16)

        def bf_3(self, a, b):
            o = self.take(a * b * 2)
            return arena[:, o:o + a * b * 2].bitcast(BF16).rearrange(
                "p (a b) -> p a b", a=a)

    P = Arena(0)
    consts_sb = P.f32(4)
    nc.const_aps.register(F32, 0.0, consts_sb[:, 0:1])
    nc.const_aps.register(F32, 1.0, consts_sb[:, 1:2])
    nc.const_aps.register(F32, 1e-5, consts_sb[:, 2:3])
    nc.const_aps.register(F32, 0.5, consts_sb[:, 3:4])
    theta_sb = P.f32(16)
    PERSIST = P.off
    STATE_OFF = (ARENA_BYTES - (4096 * 4 + 4096 * 2)) & ~63
    Ptop = Arena(STATE_OFF)
    state = Ptop.f32(4096)
    state_bf = Ptop.bf(4096)

    A1a = Arena(PERSIST)
    xfm = A1a.bf_3(16, SEG)
    A1B_BASE = A1a.off

    A1b = Arena(A1B_BASE)
    stripQ = A1b.bf(SEG)
    stripK = A1b.bf(TB)
    ang = A1b.f32(TB)
    t1 = A1b.f32(TB)
    t2 = A1b.f32(TB)
    sin_bf = A1b.bf(TB)
    cos_bf = A1b.bf(TB)
    wq_sb = [A1b.bf_3(16, 128), A1b.bf_3(16, 128)]
    kd_sb = A1b.bf(TB)
    gpos_sb = A1b.f32(TB)
    otile = [A1b.bf(TB) for _ in range(3)]

    Av = Arena(A1B_BASE)
    wv_sb = [Av.bf_3(16, 512), Av.bf_3(16, 512)]
    vout = [Av.bf(512), Av.bf(512)]

    kb.op("gpsimd", lambda g: g.dma_start(out=consts_sb, in_=consts_in),
          inc=sPRO, n=16)
    kb.op("gpsimd", lambda g: g.dma_start(out=theta_sb, in_=thetaT),
          inc=sPRO, n=16)
    for i in range(4):
        kb.op("gpsimd", lambda g, i=i: g.dma_start(
            out=xfm[:, i * 4:(i + 1) * 4, :],
            in_=x_fmT[i * 512:(i + 1) * 512, :]
                .rearrange("(a p) t -> p a t", p=128)), inc=sPRO, n=16)
    kb.barrier(ALLC)

    bank_ve = {}
    wq_te = {}
    pool_q_pe = [0, 0]
    pool_k_pe = [0]
    otile_dma = {}
    tile_rot = [0]
    ang_vr = [0]
    sc_pe = [0, 0]
    sq_ce = [0]
    gp_ce = [0]
    kd_pe = [0]
    sGP = newsem("sgp")

    a1slot = [0]

    def half_mms(wbi, pset_ignored, bank_ignored, half, ld, evac_target,
                 pool_guard):
        ev = None
        for qt2 in range(2):
            sl = a1slot[0] % 4
            a1slot[0] += 1
            pset = (pbA if sl < 2 else pbB)[:, (sl % 2) * 1024:
                                            (sl % 2) * 1024 + 1024]
            w = []
            if qt2 == 0:
                if ld is not None:
                    w.append(ld)
                if pool_guard:
                    w.append((PE, pool_guard))
            if bank_ve.get(sl):
                w.append((VE, bank_ve[sl]))
            te_last = None
            for ck in range(16):
                for q in range(2):
                    def mm(t, ck=ck, q=q, half=half, qt2=qt2, wbi=wbi,
                           pset=pset):
                        nt = half * 4 + qt2 * 2 + q
                        return t.matmul(pset[:, q * 512:(q + 1) * 512],
                                        wq_sb[wbi][:, ck, :],
                                        xfm[:, ck, nt * 512:(nt + 1) * 512],
                                        start=(ck == 0), stop=(ck == 15))
                    te_last = kb.op("tensor", mm,
                                    waits=w if (ck == 0 and q == 0) else (),
                                    inc=TE if (ck == 15 and q == 1) else None)
            wq_te[wbi] = te_last
            ev = kb.op("vector", lambda v, evac_target=evac_target, qt2=qt2,
                       pset=pset: v.tensor_copy(
                           evac_target[:, qt2 * 1024:(qt2 + 1) * 1024], pset),
                       waits=[(TE, te_last)], inc=VE)
            bank_ve[sl] = ev
        return ev

    def load_wq(col0, wbi):
        v = None
        for half in range(2):
            w = []
            if half == 0 and wq_te.get(wbi):
                w.append((TE, wq_te[wbi]))
            v = kb.op("gpsimd", lambda g, col0=col0, half=half, wbi=wbi:
                      g.dma_start(
                          out=wq_sb[wbi][:, half * 8:(half + 1) * 8, :],
                          in_=wqkvT[half * 1024:(half + 1) * 1024,
                                    col0:col0 + 128]
                              .rearrange("(a p) o -> p a o", p=128)),
                      waits=w, inc=sWQ[wbi], n=16)
        return (sWQ[wbi], v)

    def strip_write(scr, h, hf, half, buf):
        g0 = half * 4
        row0 = (2 * h + hf) * 128
        dst = scr[g0:g0 + 4, row0:row0 + 128, :].rearrange("g p t -> p g t")
        src = buf.rearrange("p (g t) -> p g t", g=4)
        return lambda e: e.dma_start(out=dst, in_=src)

    def next_tile(extra_waits):
        ti = tile_rot[0] % 3
        tile_rot[0] += 1
        w = list(extra_waits)
        if otile_dma.get(ti):
            w.append(otile_dma[ti])
        return ti, w

    for h in range(HEADS):
        ldq = load_wq(h * 128, 0)
        ldk = load_wq((16 + h) * 128, 1)
        kdl = kb.op("sync", lambda s, h=h: s.dma_start(
            out=kd_sb, in_=kd_in[h, :, :]),
            waits=[(PE, kd_pe[0])] if kd_pe[0] else (), inc=sKD[0], n=16)
        for half in range(2):
            gpl = kb.op("sync", lambda s, half=half: s.dma_start(
                out=gpos_sb, in_=gpos_in[:, half * TB:(half + 1) * TB]),
                waits=[(CE, gp_ce[0])] if gp_ce[0] else (), inc=sGP, n=16)
            c_ang = kb.op("scalar", lambda s, h=h: s.activation(
                ang, gpos_sb, AF.Copy, bias=0.0, scale=theta_sb[:, h:h + 1]),
                waits=[(sGP, gpl), (VE, ang_vr[0])], inc=CE)
            gp_ce[0] = c_ang
            kb.op("vector", lambda v: v.tensor_scalar(
                t1, ang, INV2PI, None, ALU.mult),
                waits=[(CE, c_ang), (CE, sq_ce[0])], inc=VE)
            kb.op("vector", lambda v: v.tensor_scalar(
                t2, t1, MAGIC, -MAGIC, ALU.add, ALU.add), inc=VE)
            kb.op("vector", lambda v: v.scalar_tensor_tensor(
                t1, t2, -C1, ang, ALU.mult, ALU.add), inc=VE)
            v_r = kb.op("vector", lambda v: v.scalar_tensor_tensor(
                ang, t2, -C2, t1, ALU.mult, ALU.add), inc=VE)
            ang_vr[0] = v_r
            c_sin = kb.op("scalar", lambda s: s.activation(
                sin_bf, ang, AF.Sin),
                waits=[(VE, v_r), (PE, sc_pe[0])], inc=CE)
            kb.op("scalar", lambda s: s.activation(
                t1, ang, AF.Sin, scale=0.5), inc=CE)
            c_sq = kb.op("scalar", lambda s: s.activation(
                t2, t1, AF.Square), inc=CE)
            sq_ce[0] = c_sq
            v_cos = kb.op("vector", lambda v: v.tensor_scalar(
                cos_bf, t2, -2.0, 1.0, ALU.mult, ALU.add),
                waits=[(CE, c_sq), (PE, sc_pe[1])], inc=VE)
            q_ev = half_mms(0, pbA, "qa", half, ldq if half == 0 else None,
                            stripQ[:, half * TB:(half + 1) * TB],
                            pool_q_pe[half])
            for hf, trig_bf in ((0, cos_bf), (1, sin_bf)):
                ti, w = next_tile([(VE, q_ev)])
                w.append((VE, v_cos) if hf == 0 else (CE, c_sin))
                p = kb.op("gpsimd", lambda g, ti=ti, trig_bf=trig_bf, half=half:
                          g.tensor_tensor(otile[ti],
                                          stripQ[:, half * TB:(half + 1) * TB],
                                          trig_bf, ALU.mult),
                          waits=w, inc=PE)
                otile_dma[ti] = (sOT[ti], kb.op(
                    "scalar", strip_write(q_scr, h, hf, half, otile[ti]),
                    waits=[(PE, p)], inc=sOT[ti], n=16))
            pool_q_pe[half] = PE.n
            k_ev = half_mms(1, pbB, "kb", half, ldk if half == 0 else None,
                            stripK, pool_k_pe[0])
            for hf, trig_bf in ((0, cos_bf), (1, sin_bf)):
                ti, w = next_tile([(VE, k_ev)])
                w.append((VE, v_cos) if hf == 0 else (CE, c_sin))
                p1 = kb.op("gpsimd", lambda g, ti=ti, trig_bf=trig_bf:
                           g.tensor_tensor(otile[ti], stripK, trig_bf,
                                           ALU.mult), waits=w, inc=PE)
                otile_dma[ti] = (sOT[ti], kb.op(
                    "scalar", strip_write(k_scr, h, hf, half, otile[ti]),
                    waits=[(PE, p1)], inc=sOT[ti], n=16))
                ti2, w2 = next_tile([(sKD[0], kdl)])
                p2 = kb.op("gpsimd", lambda g, ti=ti, ti2=ti2:
                           g.tensor_tensor(otile[ti2], otile[ti], kd_sb,
                                           ALU.mult),
                           waits=w2 + [(PE, p1)], inc=PE)
                otile_dma[ti2] = (sOT[ti2], kb.op(
                    "scalar", strip_write(kh_scr, h, hf, half, otile[ti2]),
                    waits=[(PE, p2)], inc=sOT[ti2], n=16))
            pool_k_pe[0] = PE.n
            sc_pe[0] = PE.n
            sc_pe[1] = PE.n
        kd_pe[0] = PE.n

    vstart = [(VE, VE.n), (CE, CE.n), (PE, PE.n)] + \
        [(c, c.n) for c in sOT if c.n]
    vload = {}
    vev = {}
    vdma = {}
    vte = {}
    for vt in range(4):
        w = list(vstart) if vt == 0 else []
        if vt >= 2:
            w.append((TE, vte[vt - 2]))
        vload[vt] = kb.op("gpsimd", lambda g, vt=vt: g.dma_start(
            out=wv_sb[vt % 2],
            in_=wqkvT[:, 4096 + vt * 512:4096 + (vt + 1) * 512]
                .rearrange("(a p) o -> p a o", p=128)),
            waits=w, inc=sWV[vt % 2], n=16)
        for tc in range(NCH):
            w = []
            if tc == 0:
                w.append((sWV[vt % 2], vload[vt]))
            bk = tc % 2
            pv = vev.get((vt, tc - 2)) or (vev.get((vt - 1, NCH - 2 + tc))
                                           if vt > 0 else bank_ve.get(bk))
            if pv:
                w.append((VE, pv))
            te_last = None
            for ck in range(16):
                def mm(t, ck=ck, bk=bk, tc=tc, vt=vt):
                    return t.matmul(pb[bk][:, :],
                                    xfm[:, ck, tc * 128:(tc + 1) * 128],
                                    wv_sb[vt % 2][:, ck, :],
                                    start=(ck == 0), stop=(ck == 15))
                te_last = kb.op("tensor", mm, waits=w if ck == 0 else (),
                                inc=TE if ck == 15 else None)
            vte[vt] = te_last
            w2 = [(TE, te_last)]
            if vdma.get((vt, tc - 2)):
                w2.append(vdma[(vt, tc - 2)])
            vev[(vt, tc)] = kb.op("vector", lambda v, tc=tc: v.tensor_copy(
                vout[tc % 2], pb[tc % 2][:, :]), waits=w2, inc=VE)
            vdma[(vt, tc)] = (sVO[tc % 2], kb.op(
                "scalar", lambda s, vt=vt, tc=tc: s.dma_start(
                    out=v_scr[tc, :, vt * 512:(vt + 1) * 512], in_=vout[tc % 2]),
                waits=[(VE, vev[(vt, tc)])], inc=sVO[tc % 2], n=16))

    kb.barrier(ALLC)
    if STOP_PHASE == "A1":
        return finish(nc, kb, ctx)

    A2 = Arena(PERSIST, STATE_OFF)
    qh_sb = [A2.bf_3(32, 256), A2.bf_3(32, 256)]
    khg_sb = [A2.bf_3(32, 256), A2.bf_3(32, 256)]
    vt_sb = [A2.bf(2048), A2.bf(2048)]
    kt_sb = [A2.bf(4096), A2.bf(4096)]
    qhat_sb = [A2.bf_3(32, 128), A2.bf_3(32, 128)]
    sT_sb = [A2.bf(128), A2.bf(128)]
    ogr_sb = [A2.bf_3(16, 512), A2.bf_3(16, 512)]
    dd_sb = A2.f32_3(16, 128)
    qd_sb = A2.bf_3(32, 128)

    ms1 = kb.op("vector", lambda v: v.memset(state, 0.0), inc=VE)
    ms2 = kb.op("vector", lambda v: v.memset(state_bf, 0.0), inc=VE)
    kb.op("sync", lambda s: s.dma_start(
        out=dd_sb, in_=dd_in.rearrange("h j i -> j h i")), inc=sMISC, n=16)
    consts_sa = (sMISC, kb.op("sync", lambda s: s.dma_start(
        out=qd_sb, in_=qd_in.rearrange("p (a b) -> p a b", a=32)),
        inc=sMISC, n=16))

    ld_sa = {}
    te_chunk = {}
    ve_stbf = {}
    ve_stt = {}
    ve_sT = {}
    ve_qhat = {}
    ce_oev = {}
    te_sc = {}
    te_o = {}
    te_kv = {}
    sb_ogrp = {}

    def prevh(d, c, h, back=2):
        if h - back >= 0:
            return d.get((c, h - back))
        if c > 0:
            return d.get((c - 1, h - back + HEADS))
        return None

    def issue_loads(c):
        g = c // 4
        if c % 2 == 0:
            hg = c // 2
            w = []
            if te_chunk.get(c - 3):
                w.append((TE, te_chunk[c - 3]))
            for scr, tgt in ((q_scr, qh_sb), (k_scr, khg_sb)):
                half = (c % 4) // 2
                for part in range(4):
                    v = kb.op("sync", lambda s, g=g, half=half, part=part,
                              scr=scr, tgt=tgt, hg=hg: s.dma_start(
                                  out=tgt[hg % 2][:, part * 8:(part + 1) * 8, :],
                                  in_=scr[g, part * 1024:(part + 1) * 1024,
                                          half * 256:(half + 1) * 256]
                                      .rearrange("(a p) t -> p a t", p=128)),
                              waits=w if part == 0 else (),
                              inc=sQK[hg % 2], n=16)
                ld_sa[("qk", hg)] = (sQK[hg % 2], v)
        w2 = []
        if te_chunk.get(c - 2):
            w2.append((TE, te_chunk[c - 2]))
        ld_sa[("v", c)] = (sVT[c % 2], kb.op("sync", lambda s, c=c: s.dma_start(
            out=vt_sb[c % 2], in_=v_scr[c, :, :]), waits=w2,
            inc=sVT[c % 2], n=16))
        ld_sa[("kt", c)] = (sKT[c % 2], kb.op("sync", lambda s, c=c: s.dma_start(
            out=kt_sb[c % 2],
            in_=kh_scr[c // 4, :, (c % 4) * 128:(c % 4 + 1) * 128],
            transpose=True), inc=sKT[c % 2], n=16))

    issue_loads(0)
    issue_loads(1)

    for c in range(NCH):
        g, hg, cin = c // 4, c // 2, c % 4
        csl = slice((c % 2) * 128, (c % 2) * 128 + 128)
        hgb = hg % 2
        wq = [ld_sa[("qk", hg)]]
        if c == 0:
            wq.append(consts_sa)
        if te_chunk.get(c - 2):
            wq.append((TE, te_chunk[c - 2]))
        ve_qhat[c] = kb.op("vector", lambda v, c=c, csl=csl, hgb=hgb:
                           v.tensor_tensor(qhat_sb[c % 2],
                                           qh_sb[hgb][:, :, csl],
                                           qd_sb, ALU.mult),
                           waits=wq, inc=VE)
        for h in range(HEADS):
            w = []
            if h == 0:
                w.append(ld_sa[("qk", hg)])
            pv = prevh(ve_sT, c, h)
            if pv:
                w.append((VE, pv))
            for half in range(2):
                def mm(t, h=h, half=half, csl=csl, hgb=hgb):
                    return t.matmul(pb[h % 2][:, 0:128],
                                    khg_sb[hgb][:, 2 * h + half, csl],
                                    qh_sb[hgb][:, 2 * h + half, csl],
                                    start=(half == 0), stop=(half == 1))
                te_sc[(c, h)] = kb.op("tensor", mm, waits=w if half == 0 else (),
                                      inc=TE if half == 1 else None)
            wv = [(TE, te_sc[(c, h)])]
            pv = prevh(te_o, c, h)
            if pv:
                wv.append((TE, pv))
            if c == 0 and h < 2:
                wv.append(consts_sa)
            ve_sT[(c, h)] = kb.op("vector", lambda v, h=h: v.tensor_tensor(
                sT_sb[h % 2], pb[h % 2][:, 0:128], dd_sb[:, h, :], ALU.mult),
                waits=wv, inc=VE)
            w = [(VE, ve_sT[(c, h)])]
            if h == 0:
                w.append(ld_sa[("v", c)])
                w.append((VE, ve_qhat[c]))
            pce = prevh(ce_oev, c, h)
            if pce:
                w.append((CE, pce))
            kb.op("tensor", lambda t, h=h, c=c: t.matmul(
                pb[2 + h % 2][:, 0:128],
                vt_sb[c % 2][:, h * 128:(h + 1) * 128],
                sT_sb[h % 2], start=True, stop=False), waits=w)
            w2 = []
            if c == 0 and h == 0:
                w2.append((VE, ms2))
            if ve_stbf.get((c - 1, h)):
                w2.append((VE, ve_stbf[(c - 1, h)]))
            for half in range(2):
                def mm(t, h=h, c=c, half=half):
                    return t.matmul(
                        pb[2 + h % 2][:, 0:128],
                        state_bf[:, h * 256 + half * 128:
                                 h * 256 + (half + 1) * 128],
                        qhat_sb[c % 2][:, 2 * h + half, :],
                        start=False, stop=(half == 1))
                te_o[(c, h)] = kb.op("tensor", mm, waits=w2 if half == 0 else (),
                                     inc=TE if half == 1 else None)
            woe = [(TE, te_o[(c, h)])]
            if cin == 0 and h == 0 and sb_ogrp.get(g - 2):
                woe.append(sb_ogrp[g - 2])
            ce_oev[(c, h)] = kb.op("scalar", lambda s, h=h, g=g, cin=cin:
                                   s.copy(ogr_sb[g % 2][:, h,
                                                        cin * 128:(cin + 1) * 128],
                                          pb[2 + h % 2][:, 0:128]),
                                   waits=woe, inc=CE)
            w = []
            if h == 0:
                w.append(ld_sa[("kt", c)])
            pv = prevh(ve_stt, c, h)
            if pv:
                w.append((VE, pv))
            for half in range(2):
                def mm(t, h=h, c=c, half=half):
                    return t.matmul(
                        pb[4 + h % 2][:, half * 128:(half + 1) * 128],
                        kt_sb[c % 2][:, (2 * h + half) * 128:
                                     (2 * h + half + 1) * 128],
                        vt_sb[c % 2][:, h * 128:(h + 1) * 128],
                        start=True, stop=True)
                te_kv[(c, h)] = kb.op("tensor", mm, waits=w if half == 0 else (),
                                      inc=TE if half == 1 else None)
            bd = float(np.exp(-SLOPES[h] * 128.0))
            wst = [(TE, te_kv[(c, h)])]
            if c == 0 and h == 0:
                wst.append((VE, ms1))
            ve_stt[(c, h)] = kb.op("vector", lambda v, h=h, bd=bd:
                                   v.scalar_tensor_tensor(
                                       state[:, h * 256:(h + 1) * 256],
                                       state[:, h * 256:(h + 1) * 256], bd,
                                       pb[4 + h % 2][:, 0:256],
                                       ALU.mult, ALU.add),
                                   waits=wst, inc=VE)
            ve_stbf[(c, h)] = kb.op("vector", lambda v, h=h: v.tensor_copy(
                state_bf[:, h * 256:(h + 1) * 256],
                state[:, h * 256:(h + 1) * 256]),
                waits=[(TE, te_o[(c, h)])], inc=VE)
        te_chunk[c] = TE.n
        if cin == 3:
            wog = [(CE, ce_oev[(c, 15)])]
            v = None
            for part in range(2):
                v = kb.op("scalar", lambda s, g=g, part=part: s.dma_start(
                    out=o_scr[g, part * 1024:(part + 1) * 1024, :]
                        .rearrange("(a p) t -> p a t", p=128),
                    in_=ogr_sb[g % 2][:, part * 8:(part + 1) * 8, :]),
                    waits=wog if part == 0 else (), inc=sOG[g % 2], n=16)
            sb_ogrp[g] = (sOG[g % 2], v)
        if c + 2 < NCH:
            issue_loads(c + 2)

    kb.barrier(ALLC)
    if STOP_PHASE == "A2":
        return finish(nc, kb, ctx)

    kb.op("sync", lambda s: s.dma_start(out=cc_in.ap(), in_=state_bf),
          inc=sCC1, n=16)
    if "dbg_state" in DEBUG_OUTS:
        dbg_state = nc.dram_tensor("dbg_state", [128, 4096], F32,
                                   kind="ExternalOutput").ap()
        kb.op("sync", lambda s: s.dma_start(out=dbg_state, in_=state),
              inc=sCC1, n=16)
    kb.op("gpsimd", lambda g: g.collective_compute(
        "AllGather", ALU.bypass,
        replica_groups=[[0, 1, 2, 3], [4, 5, 6, 7]],
        ins=[cc_in.ap().opt()], outs=[cc_gath.ap().opt()]),
        waits=[(sCC1, sCC1.n)], inc=CC)

    CA = Arena(PERSIST, STATE_OFF)
    gath_sb = [CA.bf(4096) for _ in range(4)]
    cw_sb = [CA.bf(4096) for _ in range(4)]
    tmp_cc = CA.f32(4096)

    for r in range(4):
        kb.op("sync", lambda s, r=r: s.dma_start(
            out=gath_sb[r], in_=cc_gath.ap()[r * 128:(r + 1) * 128, :]),
            waits=[(CC, 1)] if r == 0 else (), inc=sGA, n=16)
        kb.op("sync", lambda s, r=r: s.dma_start(
            out=cw_sb[r], in_=crossw_in[r, :, :]), inc=sGA, n=16)
    gl = sGA.n
    pv = kb.op("vector", lambda v: v.tensor_tensor(
        state, gath_sb[0], cw_sb[0], ALU.mult), waits=[(sGA, gl)], inc=VE)
    for r in range(1, 4):
        pv = kb.op("vector", lambda v, r=r: v.tensor_tensor(
            tmp_cc, gath_sb[r], cw_sb[r], ALU.mult), waits=[(VE, pv)], inc=VE)
        pv = kb.op("vector", lambda v: v.tensor_tensor(
            state, state, tmp_cc, ALU.add), waits=[(VE, pv)], inc=VE)
    kb.op("vector", lambda v: v.tensor_copy(state_bf, state),
          waits=[(VE, pv)], inc=VE)
    kb.barrier(ALLC)
    if STOP_PHASE == "CC":
        return finish(nc, kb, ctx)

    A3 = Arena(PERSIST, STATE_OFF)
    wpT_sb = A3.bf_3(16, 2048)
    o3_sb = A3.bf_3(16, 512)
    q3_sb = A3.bf_3(32, 512)
    wg3_sb = A3.bf_3(16, 512)
    oc_bf = [A3.bf(512), A3.bf(512)]
    x3_sb = [A3.f32(2048), A3.f32(2048)]
    r3_sb = A3.f32(2048)
    sq3_sb = A3.f32(2048)
    x13_sb = A3.f32(2048)
    x13b_sb = [A3.bf(2048), A3.bf(2048)]
    ln1 = A3.f32(12)

    for part in range(4):
        kb.op("gpsimd", lambda g, part=part: g.dma_start(
            out=wpT_sb[:, part * 4:(part + 1) * 4, :],
            in_=wprojT[part * 512:(part + 1) * 512, :]
                .rearrange("(a p) o -> p a o", p=128)), inc=sWP, n=16)
    wp_sg = sWP.n

    ocbf_pe = {}
    vr_tc = {}
    x1dma = {}
    x1bdma = [0, 0]
    ca_prev = [0]
    dbg_o3 = dbg_r = None
    if "dbg_o3" in DEBUG_OUTS:
        dbg_o3 = nc.dram_tensor("dbg_o3", [NGRP, 16, 128, 512], BF16,
                                kind="ExternalOutput").ap()
    if "dbg_r" in DEBUG_OUTS:
        dbg_r = nc.dram_tensor("dbg_r", [SEG, D], F32,
                               kind="ExternalOutput").ap()
    dbg_ln = None
    dbg_sa = [0]
    if "dbg_ln" in DEBUG_OUTS:
        dbg_ln = nc.dram_tensor("dbg_ln", [NCH, 128, 12], F32,
                                kind="ExternalOutput").ap()
    for g in range(NGRP):
        wld = []
        if g > 0:
            wld.append((TE, TE.n))
            wld.append((PE, PE.n))
        lo = kb.op("sync", lambda s, g=g: s.dma_start(
            out=o3_sb, in_=o_scr[g, :, :].rearrange("(a p) t -> p a t", p=128)),
            waits=wld, inc=sO3, n=16)
        lq = kb.op("sync", lambda s, g=g: s.dma_start(
            out=q3_sb, in_=q_scr[g, :, :].rearrange("(a p) t -> p a t", p=128)),
            inc=sQ3, n=16)
        lw = kb.op("sync", lambda s, g=g: s.dma_start(
            out=wg3_sb, in_=wglob_in[:, g, :, :].rearrange("h p j -> p h j")),
            inc=sWG3, n=16)
        pq = None
        for fc in range(32):
            pq = kb.op("gpsimd", lambda g2, fc=fc: g2.tensor_tensor(
                q3_sb[:, fc, :], q3_sb[:, fc, :], wg3_sb[:, fc // 2, :],
                ALU.mult),
                waits=[(sWG3, lw), (sQ3, lq)] if fc == 0 else (), inc=PE)
        pe_add = None
        for h in range(HEADS):
            w = [(PE, pq)] if h == 0 else []
            if ocbf_pe.get(h % 2):
                w.append((PE, ocbf_pe[h % 2]))
            tlast = None
            for half in range(2):
                def mm(t, h=h, half=half):
                    return t.matmul(
                        pb[h % 2][:, :],
                        state_bf[:, h * 256 + half * 128:
                                 h * 256 + (half + 1) * 128],
                        q3_sb[:, 2 * h + half, :],
                        start=(half == 0), stop=(half == 1))
                tlast = kb.op("tensor", mm, waits=w if half == 0 else (),
                              inc=TE if half == 1 else None)
            cev = kb.op("scalar", lambda s, h=h: s.copy(
                oc_bf[h % 2], pb[h % 2][:, :]),
                waits=[(TE, tlast), (sO3, lo)], inc=CE)
            pe_add = kb.op("gpsimd", lambda g2, h=h: g2.tensor_tensor(
                o3_sb[:, h, :], o3_sb[:, h, :], oc_bf[h % 2], ALU.add),
                waits=[(CE, cev)], inc=PE)
            ocbf_pe[h % 2] = pe_add
        if dbg_o3 is not None:
            kb.op("sync", lambda s, g=g: s.dma_start(
                out=dbg_o3[g].rearrange("a p t -> p a t"), in_=o3_sb),
                waits=[(PE, pe_add)], inc=sMISC, n=16)
        for tc in range(4):
            tok0 = g * 512 + tc * 128
            wlx = []
            if vr_tc.get(tc % 2):
                wlx.append((VE, vr_tc[tc % 2]))
            lx = kb.op("sync", lambda s, tok0=tok0: s.dma_start(
                out=x3_sb[(tok0 // 128) % 2], in_=x_tm[tok0:tok0 + 128, :]),
                waits=wlx, inc=sX3[(tok0 // 128) % 2], n=16)
            for og in range(4):
                w = []
                if og == 0 and tc == 0:
                    w += [(PE, pe_add), (sWP, wp_sg)]
                if og == 0 and vr_tc.get("last"):
                    w.append((VE, vr_tc["last"]))
                for ck in range(16):
                    def mm(t, og=og, ck=ck, tc=tc):
                        return t.matmul(pb[4 + og][:, :],
                                        o3_sb[:, ck, tc * 128:(tc + 1) * 128],
                                        wpT_sb[:, ck, og * 512:(og + 1) * 512],
                                        start=(ck == 0), stop=(ck == 15))
                    kb.op("tensor", mm, waits=w if ck == 0 else (),
                          inc=TE if ck == 15 else None)
            projte = TE.n
            vr = None
            for og in range(4):
                w = []
                if og == 0:
                    w += [(TE, projte), (sX3[(tok0 // 128) % 2], lx)]
                    if ca_prev[0]:
                        w.append((CE, ca_prev[0]))
                    if dbg_sa[0]:
                        w.append(dbg_sa[0])
                vr = kb.op("vector", lambda v, og=og, tc=tc, tok0=tok0:
                           v.tensor_tensor(
                               r3_sb[:, og * 512:(og + 1) * 512],
                               pb[4 + og][:, :],
                               x3_sb[(tok0 // 128) % 2][:, og * 512:(og + 1) * 512],
                               ALU.add), waits=w, inc=VE)
            vr_tc[tc % 2] = vr
            vr_tc["last"] = vr
            if dbg_r is not None:
                kb.op("sync", lambda s, tok0=tok0: s.dma_start(
                    out=dbg_r[tok0:tok0 + 128, :], in_=r3_sb),
                    waits=[(VE, vr)], inc=sMISC, n=16)
            def csqf(sc):
                sc.nop(cycle_cnt=1024)
                return sc.activation(sq3_sb, r3_sb, AF.Square)
            csq = kb.op("scalar", csqf, waits=[(VE, vr)], inc=CE)
            ca_prev[0] = csq
            wca = []
            if x1dma.get(0):
                wca.append((sX1, x1dma[0]))
            ca = emit_ln(kb, r3_sb, ln1, (csq, sq3_sb), x13_sb, wca)
            if dbg_ln is not None:
                dbg_sa[0] = (sMISC, kb.op("sync", lambda s, g=g, tc=tc:
                    s.dma_start(out=dbg_ln[g * 4 + tc], in_=ln1),
                    waits=[(VE, ca)], inc=sMISC, n=16))
            x1dma[0] = kb.op("scalar", lambda s, tok0=tok0: s.dma_start(
                out=x1f[tok0:tok0 + 128, :], in_=x13_sb),
                waits=[(VE, ca)], inc=sX1, n=16)
            par = (tok0 // 128) % 2
            cab = kb.op("vector", lambda v, par=par: v.tensor_copy(
                x13b_sb[par], x13_sb),
                waits=[x1bdma[par]] if x1bdma[par] else (), inc=VE)
            x1bdma[par] = (sXC if par == 0 else sX1B,
                           kb.op("scalar", lambda s, tok0=tok0, par=par:
                                 s.dma_start(out=x1b[tok0:tok0 + 128, :],
                                             in_=x13b_sb[par]),
                                 waits=[(VE, cab)],
                                 inc=sXC if par == 0 else sX1B, n=16))

    kb.barrier(ALLC)
    if STOP_PHASE == "A34":
        return finish(nc, kb, ctx)

    kb.barrier(ALLC)

    F1 = Arena(PERSIST)
    x1fm = F1.bf_3(16, SEG)
    wg_sb = [F1.bf_3(16, 128) for _ in range(2)]
    wu_sb = [F1.bf_3(16, 128) for _ in range(2)]
    g_bf = [F1.bf(1024), F1.bf(1024)]
    u_bf = [F1.bf(1024), F1.bf(1024)]
    hstrip = [F1.bf(SEG), F1.bf(SEG)]

    for fc in range(16):
        kb.op("sync", lambda s, fc=fc: s.dma_start(
            out=x1fm[:, fc, :], in_=x1b[:, fc * 128:(fc + 1) * 128],
            transpose=True), inc=sXF, n=16)
    x1fm_sa = sXF.n

    f1_te = {}
    f1_ce = {}
    f1_ve = {}
    f1_pe = {}
    f1_hdma = {}

    def f1_prev(d, hc, qt):
        if qt - 2 >= 0:
            return d.get((hc, qt - 2))
        if hc > 0:
            return d.get((hc - 1, qt + 2))
        return None

    for hc in range(64):
        w = []
        if hc >= 2:
            w.append((TE, f1_te[(hc - 2, 3)]))
        wl = None
        for wsrc, tgt in ((wgT, wg_sb), (wuT, wu_sb)):
            for half in range(2):
                wl = kb.op("gpsimd", lambda g, wsrc=wsrc, tgt=tgt, hc=hc,
                           half=half: g.dma_start(
                               out=tgt[hc % 2][:, half * 8:(half + 1) * 8, :],
                               in_=wsrc[half * 1024:(half + 1) * 1024,
                                        hc * 128:(hc + 1) * 128]
                                   .rearrange("(a p) o -> p a o", p=128)),
                           waits=w if (half == 0 and wsrc is wgT) else (),
                           inc=sFW[hc % 2], n=16)
        for qt in range(4):
            qp = qt % 2
            gsl_p = pbA[:, qp * 1024:(qp + 1) * 1024]
            usl_p = pbB[:, qp * 1024:(qp + 1) * 1024]
            w = []
            if qt == 0:
                w.append((sFW[hc % 2], wl))
                if hc == 0:
                    w.append((sXF, x1fm_sa))
            pv = f1_prev(f1_ce, hc, qt)
            if pv:
                w.append((CE, pv))
            pv = f1_prev(f1_ve, hc, qt)
            if pv:
                w.append((VE, pv))
            te_last = None
            for ck in range(16):
                for q in range(2):
                    def mmg(t, hc=hc, qt=qt, ck=ck, q=q, gsl_p=gsl_p):
                        nt = qt * 2 + q
                        return t.matmul(gsl_p[:, q * 512:(q + 1) * 512],
                                        wg_sb[hc % 2][:, ck, :],
                                        x1fm[:, ck, nt * 512:(nt + 1) * 512],
                                        start=(ck == 0), stop=(ck == 15))
                    kb.op("tensor", mmg, waits=w if (ck == 0 and q == 0) else ())
            for ck in range(16):
                for q in range(2):
                    def mmu(t, hc=hc, qt=qt, ck=ck, q=q, usl_p=usl_p):
                        nt = qt * 2 + q
                        return t.matmul(usl_p[:, q * 512:(q + 1) * 512],
                                        wu_sb[hc % 2][:, ck, :],
                                        x1fm[:, ck, nt * 512:(nt + 1) * 512],
                                        start=(ck == 0), stop=(ck == 15))
                    te_last = kb.op("tensor", mmu,
                                    inc=TE if (ck == 15 and q == 1) else None)
            f1_te[(hc, qt)] = te_last
            wsl = [(TE, te_last)]
            ppe = f1_prev(f1_pe, hc, qt)
            if ppe:
                wsl.append((PE, ppe))
            f1_ce[(hc, qt)] = kb.op("scalar", lambda s, qp=qp, gsl_p=gsl_p:
                                    s.activation(g_bf[qp], gsl_p, AF.Silu),
                                    waits=wsl, inc=CE)
            f1_ve[(hc, qt)] = kb.op("vector", lambda v, qp=qp, usl_p=usl_p:
                                    v.tensor_copy(u_bf[qp], usl_p),
                                    waits=wsl, inc=VE)
            wmul = [(CE, f1_ce[(hc, qt)]), (VE, f1_ve[(hc, qt)])]
            if qt == 0 and f1_hdma.get(hc - 2):
                wmul.append((sHW[hc % 2], f1_hdma[hc - 2]))
            f1_pe[(hc, qt)] = kb.op("gpsimd", lambda g, hc=hc, qt=qt, qp=qp:
                                    g.tensor_tensor(
                                        hstrip[hc % 2][:, qt * 1024:(qt + 1) * 1024],
                                        g_bf[qp], u_bf[qp], ALU.mult),
                                    waits=wmul, inc=PE)
        wh = [(PE, f1_pe[(hc, 3)])]
        v = None
        for part in range(4):
            v = kb.op("scalar", lambda s, hc=hc, part=part: s.dma_start(
                out=h_scr[part * 8:(part + 1) * 8, :, hc, :]
                    .rearrange("c p t -> p c t"),
                in_=hstrip[hc % 2].rearrange("p (c t) -> p c t", c=32)
                    [:, part * 8:(part + 1) * 8, :]),
                waits=wh if part == 0 else (), inc=sHW[hc % 2], n=16)
        f1_hdma[hc] = v

    kb.barrier(ALLC)
    if STOP_PHASE == "F1":
        return finish(nc, kb, ctx)

    F2 = Arena(PERSIST)
    wd_sb = F2.bf_3(32, 2048)
    h2_sb = [F2.bf_3(32, 128), F2.bf_3(32, 128)]
    pt_sb = F2.f32(2048)
    x1t_sb = [F2.f32(2048), F2.f32(2048)]
    r2_sb = F2.f32(2048)
    sq2_sb = F2.f32(2048)
    o2_sb = F2.f32(2048)
    ln2 = F2.f32(12)

    f2_te = {}
    f2_ev = {}
    pt_dma = {}
    out_dma = [0]
    csq2_prev = [0]
    for Hh in range(2):
        w = []
        if Hh == 1:
            w.append((TE, TE.n))
        wd_parts = []
        for part in range(4):
            wd_parts.append(kb.op(
                "gpsimd", lambda g, Hh=Hh, part=part: g.dma_start(
                    out=wd_sb[:, part * 8:(part + 1) * 8, :],
                    in_=wdT[Hh * 4096 + part * 1024:
                            Hh * 4096 + (part + 1) * 1024, :]
                        .rearrange("(a p) o -> p a o", p=128)),
                waits=w if part == 0 else (), inc=sWD, n=16))
        wdl = wd_parts[-1]
        for tc in range(NCH):
            w = []
            hprev = f2_te.get((Hh, tc - 2)) or \
                (f2_te.get((0, NCH - 2 + tc)) if Hh == 1 else None)
            if hprev:
                w.append((TE, hprev))
            lh = kb.op("sync", lambda s, Hh=Hh, tc=tc: s.dma_start(
                out=h2_sb[tc % 2], in_=h_scr[tc, :, Hh * 32:(Hh + 1) * 32, :]),
                waits=w, inc=sH2[tc % 2], n=16)
            wlx = []
            xprev = f2_ev.get((Hh, tc - 2)) or \
                (f2_ev.get((0, NCH - 2 + tc)) if Hh == 1 else None)
            if xprev:
                wlx.append((VE, xprev))
            if Hh == 0:
                lx = kb.op("sync", lambda s, tc=tc: s.dma_start(
                    out=x1t_sb[tc % 2], in_=x1f[tc * 128:(tc + 1) * 128, :]),
                    waits=wlx, inc=sLX[tc % 2], n=16)
            else:
                lx = kb.op("sync", lambda s, tc=tc: s.dma_start(
                    out=x1t_sb[tc % 2], in_=part_scr[tc, :, :]),
                    waits=wlx + [pt_dma[tc]], inc=sLX[tc % 2], n=16)
            w = [(sH2[tc % 2], lh)]
            pvv = f2_ev.get((Hh, tc - 1)) if tc > 0 else \
                (f2_ev.get((0, NCH - 1)) if Hh == 1 else None)
            if pvv:
                w.append((VE, pvv))
            tl = None
            for hc in range(32):
                mmw = ()
                if hc == 0 and tc == 0:
                    mmw = w[:]
                if tc == 0 and hc % 8 == 0:
                    mmw = list(mmw) + [(sWD, wd_parts[hc // 8])]
                elif hc == 0:
                    mmw = w
                for og in range(4):
                    def mm(t, hc=hc, og=og, tc=tc):
                        return t.matmul(pb[4 + og][:, :], h2_sb[tc % 2][:, hc, :],
                                        wd_sb[:, hc, og * 512:(og + 1) * 512],
                                        start=(hc == 0), stop=(hc == 31))
                    tl = kb.op("tensor", mm,
                               waits=mmw if og == 0 else (),
                               inc=TE if (hc == 31 and og == 3) else None)
            f2_te[(Hh, tc)] = tl
            if Hh == 0:
                vv = None
                for og in range(4):
                    w = []
                    if og == 0:
                        w = [(TE, tl), (sLX[tc % 2], lx)]
                        if pt_dma.get(tc - 1):
                            w.append(pt_dma[tc - 1])
                    vv = kb.op("vector", lambda v, og=og, tc=tc: v.tensor_tensor(
                        pt_sb[:, og * 512:(og + 1) * 512], pb[4 + og][:, :],
                        x1t_sb[tc % 2][:, og * 512:(og + 1) * 512], ALU.add),
                        waits=w, inc=VE)
                f2_ev[(Hh, tc)] = vv
                pt_dma[tc] = (sPT[tc % 2], kb.op(
                    "scalar", lambda s, tc=tc: s.dma_start(
                        out=part_scr[tc, :, :], in_=pt_sb), waits=[(VE, vv)],
                    inc=sPT[tc % 2], n=16))
            else:
                vv = None
                for og in range(4):
                    w = []
                    if og == 0:
                        w = [(TE, tl), (sLX[tc % 2], lx)]
                        if csq2_prev[0]:
                            w.append((CE, csq2_prev[0]))
                    vv = kb.op("vector", lambda v, og=og, tc=tc: v.tensor_tensor(
                        r2_sb[:, og * 512:(og + 1) * 512], pb[4 + og][:, :],
                        x1t_sb[tc % 2][:, og * 512:(og + 1) * 512], ALU.add),
                        waits=w, inc=VE)
                f2_ev[(Hh, tc)] = vv
                def csqf2(sc):
                    sc.nop(cycle_cnt=1024)
                    return sc.activation(sq2_sb, r2_sb, AF.Square)
                csq = kb.op("scalar", csqf2, waits=[(VE, vv)], inc=CE)
                csq2_prev[0] = csq
                wca = []
                if out_dma[0]:
                    wca.append((OQ, out_dma[0]))
                ca2 = emit_ln(kb, r2_sb, ln2, (csq, sq2_sb), o2_sb, wca)
                out_dma[0] = kb.op("scalar", lambda s, tc=tc: s.dma_start(
                    out=out[tc * 128:(tc + 1) * 128, :], in_=o2_sb),
                    waits=[(VE, ca2)], inc=OQ, n=16)

    kb.op("sync", lambda s: None, waits=[(OQ, OQ.n)])
    kb.op("scalar", lambda s: None, waits=[(OQ, OQ.n)])
    return finish(nc, kb, ctx)


def finish(nc, kb, ctx):
    with nc.Block() as block:
        @block.tensor
        def _(t):
            for f in kb.ops["tensor"]:
                f(t)

        @block.vector
        def _(v):
            for f in kb.ops["vector"]:
                f(v)

        @block.scalar
        def _(s):
            for f in kb.ops["scalar"]:
                f(s)

        @block.gpsimd
        def _(g):
            for f in kb.ops["gpsimd"]:
                f(g)

        @block.sync
        def _(s):
            for f in kb.ops["sync"]:
                f(s)
    ctx.close()
    return nc


def host_consts(seg_idx):
    s = SLOPES
    jj = np.arange(128, dtype=np.float64)
    qd = np.zeros((32, 128), dtype=np.float64)
    for fc in range(32):
        qd[fc] = np.exp(-s[fc // 2] * (jj + 1.0))
    qd = np.tile(qd.reshape(1, 32 * 128), (128, 1)).astype(BF)

    j2k = np.arange(2048, dtype=np.float64) % 128
    kd = np.zeros((16, 128, 2048), dtype=np.float64)
    for h in range(16):
        kd[h, :, :] = np.exp(-s[h] * (127.0 - j2k))[None, :]
    kd = kd.astype(BF)

    dd = np.zeros((16, 128, 128), dtype=np.float64)
    ii = np.arange(128, dtype=np.float64)
    for h in range(16):
        dif = ii[None, :] - jj[:, None]
        dd[h] = np.where(dif >= 0, np.exp(-s[h] * dif), 0.0)
    dd = dd.astype(np.float32)

    wg = np.zeros((16, NGRP, 128, 512), dtype=np.float64)
    jg = np.arange(512, dtype=np.float64)
    for h in range(16):
        for g in range(NGRP):
            wg[h, g, :, :] = np.exp(-s[h] * (g * 512.0 + jg + 1.0))[None, :]
    wglob = wg.astype(BF)

    cw = np.zeros((4, 128, 4096), dtype=np.float64)
    for sp in range(4):
        if sp < seg_idx:
            for h in range(16):
                cw[sp, :, h * 256:(h + 1) * 256] = np.exp(
                    -4096.0 * s[h] * (seg_idx - sp - 1))
    crossw = cw.astype(BF)

    gp = (seg_idx * 4096.0 + np.arange(4096, dtype=np.float64))
    gpos = np.tile(gp.reshape(1, -1), (128, 1)).astype(np.float32)

    consts = np.zeros((128, 4), dtype=np.float32)
    consts[:, 1] = 1.0
    consts[:, 2] = 1e-5
    consts[:, 3] = 0.5
    return dict(qd=np.ascontiguousarray(qd.reshape(128, 4096)), kd=kd, dd=dd,
                wglob=wglob, crossw=crossw, gpos=gpos, consts=consts)


_NC_CACHE = {}


def make_in_maps(L0_attn_x, L0_attn_w_qkv, L0_attn_w_proj, L0_attn_theta,
                 L0_ffn_w_gate, L0_ffn_w_up, L0_ffn_w_down):
    x = np.asarray(L0_attn_x, dtype=np.float32)
    wq = np.ascontiguousarray(np.asarray(L0_attn_w_qkv, np.float32).T)
    wp = np.ascontiguousarray(np.asarray(L0_attn_w_proj, np.float32).T)
    th = np.ascontiguousarray(
        np.asarray(L0_attn_theta, np.float32).reshape(16, 128).T)
    wgm = np.ascontiguousarray(np.asarray(L0_ffn_w_gate, np.float32).T)
    wum = np.ascontiguousarray(np.asarray(L0_ffn_w_up, np.float32).T)
    wdm = np.ascontiguousarray(np.asarray(L0_ffn_w_down, np.float32).T)
    in_maps = []
    for i in range(8):
        b, sgi = i // 4, i % 4
        xs = np.ascontiguousarray(x[b, sgi * SEG:(sgi + 1) * SEG, :])
        m = dict(x_tm=xs, x_fmT=np.ascontiguousarray(xs.T),
                 wqkvT=wq, wprojT=wp, wgT=wgm, wuT=wum, wdT=wdm, thetaT=th)
        m.update(host_consts(sgi))
        in_maps.append(m)
    return in_maps


def kernel(L0_attn_x, L0_attn_w_qkv, L0_attn_w_proj, L0_attn_theta,
           L0_ffn_w_gate, L0_ffn_w_up, L0_ffn_w_down):
    key = (STOP_PHASE, tuple(sorted(DEBUG_OUTS)))
    if key not in _NC_CACHE:
        _NC_CACHE[key] = build_nc()
    nc = _NC_CACHE[key]
    in_maps = make_in_maps(L0_attn_x, L0_attn_w_qkv, L0_attn_w_proj,
                           L0_attn_theta, L0_ffn_w_gate, L0_ffn_w_up,
                           L0_ffn_w_down)
    res = run_bass_kernel_spmd(nc, in_maps, core_ids=list(range(8)))
    full = np.empty((B, T, D), dtype=np.float32)
    for i in range(8):
        b, sgi = i // 4, i % 4
        full[b, sgi * SEG:(sgi + 1) * SEG, :] = res.results[i]["out"]
    return full
